# version 07
# baseline (speedup 1.0000x reference)
import contextlib

import numpy as np
import ml_dtypes

import concourse.bass as bass
import concourse.mybir as mybir
from concourse.bass_utils import run_bass_kernel_spmd

F32 = mybir.dt.float32
BF16 = mybir.dt.bfloat16
AF = mybir.ActivationFunctionType
ALU = mybir.AluOpType
AX = mybir.AxisListType

DM, DI, NH, HD, DS = 2048, 4096, 64, 64, 128
CHK, NCHK, CVD, DIP, RT = 256, 8, 4352, 8512, 2048
NKT, NE, TC, NTC = 16, 4, 512, 4
EPS = 1e-5
BF = ml_dtypes.bfloat16

PHASES = ("ffn1", "halo", "ln2", "inproj", "s1", "s2", "outproj", "ffn2")
DBG_DUMP = None
DBG_FFN = False


_OPS = {
    "matmul", "transpose", "ldweights", "activation", "copy", "add", "mul",
    "tensor_copy", "tensor_tensor", "tensor_add", "tensor_sub", "tensor_mul",
    "tensor_max", "tensor_scalar", "tensor_scalar_add", "tensor_scalar_sub",
    "tensor_scalar_mul", "tensor_scalar_max", "tensor_scalar_min",
    "scalar_tensor_tensor", "tensor_tensor_scan", "reduce_max", "reduce_sum",
    "tensor_reduce", "reciprocal", "select", "memset", "stream_shuffle",
    "dma_start", "collective_compute",
}


class EngW:
    def __init__(self, e):
        self.e = e
        self.last = None

    def __getattr__(self, name):
        a = getattr(self.e, name)
        if name in _OPS:
            def wrap(*args, **kw):
                r = a(*args, **kw)
                self.last = r
                return r
            return wrap
        return a


class Ctr:
    def __init__(self, es, nc, name):
        self.h = es.enter_context(nc.semaphore(name))
        self.n = 0

    def inc(self, instr, k=1):
        instr.then_inc(self.h, k)
        self.n += k
        return self.n


def build_nc():
    nc = bass.Bass("TRN2", target_bir_lowering=False)
    es = contextlib.ExitStack()

    dt_ = nc.dram_tensor
    u_ext = dt_("u_ext", [DM, RT], F32, kind="ExternalInput").ap()
    w_ip = dt_("w_ip", [DM, DIP], F32, kind="ExternalInput").ap()
    w_op = dt_("w_op", [DI, DM], F32, kind="ExternalInput").ap()
    w1_ext = [dt_(f"w1_{e}", [DM, DM], F32, kind="ExternalInput").ap() for e in range(NE)]
    w2_ext = [dt_(f"w2_{e}", [DM, DM], F32, kind="ExternalInput").ap() for e in range(NE)]
    cst_f_ext = dt_("cst_f", [128, 1024], F32, kind="ExternalInput").ap()
    cst_f2_ext = dt_("cst_f2", [128, 640], F32, kind="ExternalInput").ap()
    cst_b_ext = dt_("cst_b", [128, 8192], BF16, kind="ExternalInput").ap()
    b2_ext = dt_("b2_ext", [4, DM], F32, kind="ExternalInput").ap()
    out_ext = dt_("out_ext", [DM, RT], F32, kind="ExternalOutput").ap()
    if DBG_FFN:
        dbg_stat = dt_("dbg_stat", [128, 1536], F32, kind="ExternalOutput").ap()
        dbg_xln = dt_("dbg_xln", [128, 1024], F32, kind="ExternalOutput").ap()
        dbg_wbc = dt_("dbg_wbc", [128, 2048], F32, kind="ExternalOutput").ap()
        dbg_h1w = dt_("dbg_h1w", [128, 1024], F32, kind="ExternalOutput").ap()
        dbg_eL = dt_("dbg_eL", [128, 32], F32, kind="ExternalOutput").ap()
        dbg_xln2 = dt_("dbg_xln2", [128, 1024], F32, kind="ExternalOutput").ap()
        dbg_lg = dt_("dbg_lg", [128, 16], F32, kind="ExternalOutput").ap()
        dbg_nmc = dt_("dbg_nmc", [128, 16], F32, kind="ExternalOutput").ap()

    w1c = dt_("w1c_dram", [128, 64 * 2048], BF16).ap()
    w2c = dt_("w2c_dram", [128, 16 * 8192], BF16).ap()
    x_dram = dt_("x_dram", [DM, RT], F32).ap()
    z_dram = dt_("z_dram", [DI, RT], BF16).ap()
    xcb_dram = dt_("xcb_dram", [CVD, RT], BF16).ap()
    xt_dram = dt_("xt_dram", [RT, DI], BF16).ap()
    st_dram = dt_("st_dram", [NCHK * 128, DI], BF16).ap()
    yn_dram = dt_("yn_dram", [DI, RT], BF16).ap()
    ac_dram = dt_("ac_dram", [NH, CHK], F32).ap()
    ea_dram = dt_("ea_dram", [NH, CHK], F32).ap()
    hcc_in = dt_("hcc_in", [128, 48], F32)
    hcc_out = dt_("hcc_out", [256, 48], F32)
    scc_in = dt_("scc_in", [128, DI], F32)
    scc_out = dt_("scc_out", [256, DI], F32)

    C = lambda n: Ctr(es, nc, n)
    c_ld = C("c_ld")
    c_ldx = C("c_ldx")
    c_stw = C("c_stw")
    c_sw2 = C("c_sw2")
    c_st = C("c_st")
    c_sta = C("c_sta")
    c_stb = C("c_stb")
    c_lg = C("c_lg")
    c_lga = C("c_lga")
    c_lgb = C("c_lgb")
    c_pe = C("c_pe")
    c_se = C("c_se")
    c_ve = C("c_ve")
    c_evA = C("c_evA")
    c_cc1 = C("c_cc1")
    c_cc2 = C("c_cc2")
    c_bar = C("c_bar")
    bar_k = [0]

    sb = lambda n, s, d=F32: es.enter_context(nc.sbuf_tensor(n, s, d))
    cst_f = sb("cst_fS", [128, 1024], F32)
    cst_f2 = sb("cst_f2S", [128, 640], F32)
    cstb0 = sb("cstb0S", [128, 1024], BF16)
    identb_ffn = sb("identb_ffn", [128, 128], BF16)
    b2_b = sb("b2_b", [4, DM], BF16)
    gateT_b = sb("gateT_b", [128, 64], BF16)
    acol = sb("acol", [64, 1], F32)
    xhalo = sb("xhalo", [128, 48], F32)
    xln2h = sb("xln2h", [128, 48], BF16)

    eps_col = cst_f[:, 2:3]
    one_col64 = cst_f[0:64, 3:4]
    convw = lambda t, k: cst_f[:, 8 + t * 4 + k:9 + t * 4 + k]
    convb = lambda t: cst_f[:, 144 + t:145 + t]
    b1col = lambda i: cst_f[:, 178 + i:179 + i]
    gateT_f = cst_f[:, 242:306]
    dtb_col = cst_f[0:64, 306:307]
    alog_col = cst_f[0:64, 307:308]
    drep = lambda t: cst_f[:, 308 + t:309 + t]
    nwcol = lambda t: cst_f[:, 340 + t:341 + t]
    smask = cst_f[:, 372:373]
    onesr_f = cst_f[0:1, 374:502]
    ident_f = cst_f[:, 512:640]
    dmask_f = cst_f2[:, 0:512]
    ones4 = lambda qd: cst_f2[32 * (qd & 1):32 * (qd & 1) + 1, 512:640]

    sel4_b = lambda e: cstb0[0:4, e * 128:(e + 1) * 128]
    onesb4 = lambda qd: cstb0[32 * (qd & 1):32 * (qd & 1) + 1, 896:1024]
    onesS_b = cstb0[:, 640:768]
    onesSi_b = cstb0[:, 768:896]

    pf = [es.enter_context(nc.psum_tensor(f"pf{i}", [128, 512], F32)) for i in range(6)]
    pb = [es.enter_context(nc.psum_tensor(f"pb{i}", [128, 1024], BF16)) for i in range(2)]

    block = es.enter_context(nc.Block())
    V, S, T, Y, G = "vector", "scalar", "tensor", "sync", "gpsimd"

    def sec(e):
        raw = getattr(block, e)

        def deco(fn):
            return raw(lambda eng, _f=fn: _f(EngW(eng)))
        return deco

    _own = {"tensor": None, "vector": None, "scalar": None}

    def barrier():
        bar_k[0] += 1
        tgt = 5 * bar_k[0]
        drains = {T: [(c_pe, c_pe.n)], V: [(c_ve, c_ve.n), (c_evA, c_evA.n)],
                  S: [(c_se, c_se.n)], Y: [], G: []}
        for name in (T, V, S, Y, G):
            @sec(name)
            def _(eng, _t=tgt, _d=drains[name]):
                for ctr, val in _d:
                    eng.wait_ge(ctr.h, val)
                eng.nop().then_inc(c_bar.h, 1)
                eng.wait_ge(c_bar.h, _t)
        c_bar.n = tgt

    TP0 = (0, 0)

    @sec(Y)
    def _(sy):
        c_ld.inc(sy.dma_start(out=cst_f[:], in_=cst_f_ext), 16)
        c_ld.inc(sy.dma_start(out=cst_f2[:], in_=cst_f2_ext), 16)
        c_ld.inc(sy.dma_start(out=cstb0[:], in_=cst_b_ext[:, 0:1024]), 16)
        c_ld.inc(sy.dma_start(out=identb_ffn[:], in_=cst_b_ext[:, 1024:1152]), 16)
    ld0 = c_ld.n

    @sec(G)
    def _(g):
        c_lg.inc(g.dma_start(out=b2_b[:], in_=b2_ext, max_dma_last_dim=4096), 16)
        g.wait_ge(c_lg.h, c_lg.n)

    @sec(S)
    def _(s):
        s.wait_ge(c_ld.h, ld0)
        c_se.inc(s.activation(out=acol[:], in_=alog_col, func=AF.Exp))
    se0 = c_se.n

    @sec(V)
    def _(v):
        v.wait_ge(c_se.h, se0)
        v.tensor_scalar(out=acol[:], in0=acol[:], scalar1=-1.0, scalar2=None,
                        op0=ALU.mult)
        c_ve.inc(v.tensor_copy(out=gateT_b[:], in_=gateT_f))
    barrier()

    def emit_ffn(src, dst, tag):
        ph = contextlib.ExitStack()
        psb = lambda n, s, d=F32: ph.enter_context(nc.sbuf_tensor(n + tag, s, d))
        x_b = psb("x_b", [128, NKT * TC], BF16)
        x_fo = psb("x_fo", [128, 4 * TC])
        xsq = psb("xsq", [128, NKT * TC], BF16)
        xln = psb("xln", [128, NKT * TC], BF16)
        h1s = [psb(f"h1s{i}", [128, TC], BF16) for i in range(2)]
        h1w = psb("h1w", [128, 64 * TC], BF16)
        w1b = [psb(f"w1b{i}", [128, NKT * 128], BF16) for i in range(2)]
        w2b = [psb(f"w2b{i}", [128, 64 * 128], BF16) for i in range(2)]
        stat = psb("stat", [128, 1536])
        MUB = stat[:, 0:512]
        BB = stat[:, 512:1024]
        CC = stat[:, 1024:1536]
        eL = psb("eL", [128, 16])
        nmc = psb("nmc", [128, 4])
        wfT32 = psb("wfT32", [128, 128], BF16)
        t8 = psb("t8", [128, 8])
        t16 = psb("t16", [128, 16])
        g8h = psb("g8h", [128, 8])
        thrq = psb("thrq", [128, 4])
        top1q = psb("top1q", [128, 4])
        giq = psb("giq", [128, 16])
        summq = psb("summq", [128, 4])
        rec4 = psb("rec4", [128, 4])
        wn16 = psb("wn16", [128, 16])
        wrow = psb("wrow", [4, 512], BF16)
        wbc = psb("wbc", [128, NE * TC], BF16)
        outs2 = psb("outs2", [128, 2 * TC])

        w1rel = [0, 0]
        w2rel = [0, 0]
        x_rel = 0
        for tc in range(NTC):
            cc0 = tc * TC

            @sec(G)
            def _(g, cc0=cc0, xr=x_rel):
                g.wait_ge(c_ve.h, xr)
                g.wait_ge(c_lg.h, c_lg.n)
                c_lg.inc(g.dma_start(
                    out=x_b[:],
                    in_=src[:, cc0:cc0 + TC].rearrange("(kt p) t -> p kt t", p=128)), 16)
            ldxb_r = c_lg.n

            @sec(S)
            def _(s, xr=x_rel):
                s.wait_ge(c_lg.h, ldxb_r)
                s.wait_ge(c_ve.h, xr)
                c_se.inc(s.activation(out=xsq[:], in_=x_b[:], func=AF.Square))
            se_sq = c_se.n

            @sec(T)
            def _(t, xr=x_rel):
                t.wait_ge(c_lg.h, ldxb_r)
                t.wait_ge(c_se.h, se_sq)
                t.wait_ge(c_ve.h, xr)
                for kt in range(NKT):
                    t.matmul(pf[0][:], onesS_b, x_b[:, kt * TC:(kt + 1) * TC],
                             start=(kt == 0), stop=(kt == NKT - 1))
                for kt in range(NKT):
                    t.matmul(pf[1][:], onesS_b, xsq[:, kt * TC:(kt + 1) * TC],
                             start=(kt == 0), stop=(kt == NKT - 1))
                c_pe.inc(t.last)
            pe_st = c_pe.n

            @sec(V)
            def _(v):
                v.wait_ge(c_pe.h, pe_st)
                v.tensor_copy(out=MUB, in_=pf[0][:])
                v.scalar_tensor_tensor(out=BB, in0=MUB, scalar=-1.0, in1=MUB,
                                       op0=ALU.mult, op1=ALU.mult)
                c_ve.inc(v.tensor_tensor(out=BB, in0=BB, in1=pf[1][:], op=ALU.add))
            ve_var = c_ve.n

            @sec(S)
            def _(s):
                s.wait_ge(c_ve.h, ve_var)
                s.activation(out=CC, in_=BB, func=AF.Ln, bias=eps_col)
                c_se.inc(s.activation(out=BB, in_=CC, func=AF.Exp, scale=-0.5))
            se_rstd = c_se.n

            @sec(V)
            def _(v):
                v.wait_ge(c_se.h, se_rstd)
                c_ve.inc(v.scalar_tensor_tensor(out=CC, in0=MUB, scalar=-1.0,
                                                in1=BB, op0=ALU.mult, op1=ALU.mult))
            ve_c0 = c_ve.n

            @sec(V)
            def _(v):
                v.wait_ge(c_ve.h, ve_c0)
                for kt in range(NKT):
                    v.tensor_tensor(out=xln[:, kt * TC:(kt + 1) * TC],
                                    in0=x_b[:, kt * TC:(kt + 1) * TC],
                                    in1=BB, op=ALU.mult)
                for kt in range(NKT):
                    v.tensor_add(xln[:, kt * TC:(kt + 1) * TC],
                                 xln[:, kt * TC:(kt + 1) * TC], CC)
                c_ve.inc(v.last)
            ve_xln = c_ve.n

            @sec(T)
            def _(t):
                t.wait_ge(c_ve.h, ve_xln)
                for ti in range(4):
                    for kt in range(NKT):
                        t.matmul(pf[2][:, ti * 4:(ti + 1) * 4],
                                 xln[:, kt * TC + ti * 128:kt * TC + (ti + 1) * 128],
                                 gateT_b[:, kt * 4:(kt + 1) * 4],
                                 start=(ti == 0 and kt == 0),
                                 stop=(ti == 3 and kt == NKT - 1))
                c_pe.inc(t.last)
            pe_g = c_pe.n

            @sec(V)
            def _(v):
                v.wait_ge(c_pe.h, pe_g)
                for i in range(4):
                    v.reduce_max(out=t8[:, i:i + 1], in_=pf[2][:, i * 4:(i + 1) * 4],
                                 axis=AX.X)
                c_ve.inc(v.last)
                v.wait_ge(c_ve.h, c_ve.n)
                c_ve.inc(v.tensor_scalar(out=nmc[:], in0=t8[:, 0:4],
                                         scalar1=-1.0, scalar2=None, op0=ALU.mult))
            ve_nm = c_ve.n

            @sec(S)
            def _(s):
                s.wait_ge(c_ve.h, ve_nm)
                for i in range(4):
                    s.activation(out=eL[:, i * 4:(i + 1) * 4],
                                 in_=pf[2][:, i * 4:(i + 1) * 4],
                                 func=AF.Exp, bias=nmc[:, i:i + 1])
                c_se.inc(s.last)
            se_eL = c_se.n

            @sec(V)
            def _(v):
                v.wait_ge(c_se.h, se_eL)
                e = eL
                for i in range(4):
                    v.tensor_tensor(out=t16[:, i:i + 1], in0=e[:, 4 * i:4 * i + 1],
                                    in1=e[:, 4 * i + 1:4 * i + 2], op=ALU.max)
                for i in range(4):
                    v.tensor_tensor(out=t16[:, 8 + i:9 + i], in0=e[:, 4 * i + 2:4 * i + 3],
                                    in1=e[:, 4 * i + 3:4 * i + 4], op=ALU.max)
                for i in range(4):
                    v.tensor_tensor(out=t16[:, 4 + i:5 + i], in0=e[:, 4 * i:4 * i + 1],
                                    in1=e[:, 4 * i + 1:4 * i + 2], op=ALU.min)
                for i in range(4):
                    v.tensor_tensor(out=t16[:, 12 + i:13 + i], in0=e[:, 4 * i + 2:4 * i + 3],
                                    in1=e[:, 4 * i + 3:4 * i + 4], op=ALU.min)
                c_ve.inc(v.last)
                v.wait_ge(c_ve.h, c_ve.n)
                for i in range(4):
                    v.tensor_tensor(out=g8h[:, i:i + 1], in0=t16[:, i:i + 1],
                                    in1=t16[:, 8 + i:9 + i], op=ALU.min)
                for i in range(4):
                    v.tensor_tensor(out=g8h[:, 4 + i:5 + i], in0=t16[:, 4 + i:5 + i],
                                    in1=t16[:, 12 + i:13 + i], op=ALU.max)
                for i in range(4):
                    v.tensor_tensor(out=top1q[:, i:i + 1], in0=t16[:, i:i + 1],
                                    in1=t16[:, 8 + i:9 + i], op=ALU.max)
                c_ve.inc(v.last)
                v.wait_ge(c_ve.h, c_ve.n)
                for i in range(4):
                    v.tensor_tensor(out=thrq[:, i:i + 1], in0=g8h[:, i:i + 1],
                                    in1=g8h[:, 4 + i:5 + i], op=ALU.max)
                c_ve.inc(v.last)
                v.wait_ge(c_ve.h, c_ve.n)
                for i in range(4):
                    v.tensor_scalar(out=giq[:, 4 * i:4 * i + 4],
                                    in0=eL[:, 4 * i:4 * i + 4],
                                    scalar1=thrq[:, i:i + 1], scalar2=None,
                                    op0=ALU.is_ge)
                v.tensor_tensor(out=summq[:], in0=top1q[:], in1=thrq[:], op=ALU.add)
                c_ve.inc(v.last)
                v.wait_ge(c_ve.h, c_ve.n)
                v.reciprocal(out=rec4[:], in_=summq[:])
                v.tensor_tensor(out=wn16[:], in0=giq[:], in1=eL[:], op=ALU.mult)
                c_ve.inc(v.last)
                v.wait_ge(c_ve.h, c_ve.n)
                for i in range(4):
                    v.tensor_scalar(out=wfT32[:, i * 32:i * 32 + 4],
                                    in0=wn16[:, 4 * i:4 * i + 4],
                                    scalar1=rec4[:, i:i + 1], scalar2=0.5,
                                    op0=ALU.mult, op1=ALU.mult)
                c_ve.inc(v.last)
            ve_wf = c_ve.n

            @sec(T)
            def _(t):
                t.wait_ge(c_ve.h, ve_wf)
                for i in range(4):
                    t.transpose(pb[0][0:32, i * 128:(i + 1) * 128],
                                wfT32[:, i * 32:(i + 1) * 32], identb_ffn[:])
                c_pe.inc(t.last)
            pe_wr = c_pe.n

            @sec(V)
            def _(v):
                v.wait_ge(c_pe.h, pe_wr)
                c_ve.inc(v.tensor_copy(out=wrow[:], in_=pb[0][0:4, 0:512]))
            ve_wrow = c_ve.n

            pe_wbc0 = c_pe.n
            ve_wbc0 = c_ve.n

            @sec(T)
            def _(t):
                t.wait_ge(c_ve.h, ve_wrow)
                for e in range(NE):
                    if e > 0:
                        t.wait_ge(c_ve.h, ve_wbc0 + e)
                    t.matmul(pf[5][:], sel4_b(e), wrow[:],
                             start=True, stop=True, tile_position=TP0)
                    c_pe.inc(t.last)

            @sec(V)
            def _(v):
                for e in range(NE):
                    v.wait_ge(c_pe.h, pe_wbc0 + e + 1)
                    c_ve.inc(v.tensor_copy(out=wbc[:, e * TC:(e + 1) * TC],
                                           in_=pf[5][:]))
            ve_wbc = c_ve.n

            pe_m1 = c_pe.n
            se_m1 = c_se.n
            ve_m1 = c_ve.n

            lw1tot = []
            cache_wr = (tag == "F1" and tc == 0)
            cache_rd = (tag == "F2") or (tag == "F1" and tc >= 1)
            stw_m1_0 = c_stw.n
            stw_all = c_stw.n
            sw2_all = c_sw2.n

            @sec(G)
            def _(g):
                for i in range(NE * NKT):
                    e, ht = divmod(i, NKT)
                    par = c_lga if i % 2 == 0 else c_lgb
                    if i >= 2:
                        g.wait_ge(c_pe.h, pe_m1 + i - 1)
                        if cache_wr:
                            g.wait_ge(c_stw.h, stw_m1_0 + 16 * (i - 1))
                    else:
                        g.wait_ge(c_pe.h, w1rel[i % 2])
                        if cache_rd and tag == "F1" and tc == 1:
                            g.wait_ge(c_stw.h, stw_all)
                            g.wait_ge(c_sw2.h, sw2_all)
                    g.wait_ge(par.h, par.n)
                    if cache_rd:
                        srcw = w1c[:, i * 2048:(i + 1) * 2048]
                    else:
                        srcw = w1_ext[e][:, ht * 128:(ht + 1) * 128]
                        srcw = srcw.rearrange("(kt p) c -> p kt c", p=128)
                    par.inc(g.dma_start(out=w1b[i % 2][:], in_=srcw), 16)
                    lw1tot.append((par, par.n))

            if cache_wr:
                @sec(Y)
                def _(sy):
                    for i in range(NE * NKT):
                        sy.wait_ge(lw1tot[i][0].h, lw1tot[i][1])
                        c_stw.inc(sy.dma_start(out=w1c[:, i * 2048:(i + 1) * 2048],
                                               in_=w1b[i % 2][:]), 16)

            @sec(T)
            def _(t):
                t.wait_ge(c_ve.h, ve_wbc)
                for i in range(NE * NKT):
                    t.wait_ge(lw1tot[i][0].h, lw1tot[i][1])
                    if i >= 2:
                        t.wait_ge(c_se.h, se_m1 + i - 1)
                    for kt in range(NKT):
                        t.matmul(pf[i % 2][:],
                                 w1b[i % 2][:, kt * 128:(kt + 1) * 128],
                                 xln[:, kt * TC:(kt + 1) * TC],
                                 start=(kt == 0), stop=(kt == NKT - 1))
                    c_pe.inc(t.last)
                    w1rel[i % 2] = c_pe.n

            @sec(S)
            def _(s):
                for i in range(NE * NKT):
                    s.wait_ge(c_pe.h, pe_m1 + i + 1)
                    if i >= 2:
                        s.wait_ge(c_ve.h, ve_m1 + i - 1)
                    c_se.inc(s.activation(out=h1s[i % 2][:], in_=pf[i % 2][:],
                                          func=AF.Gelu, bias=b1col(i)))

            @sec(V)
            def _(v):
                for i in range(NE * NKT):
                    e = i // NKT
                    v.wait_ge(c_se.h, se_m1 + i + 1)
                    c_ve.inc(v.tensor_tensor(out=h1w[:, i * TC:(i + 1) * TC],
                                             in0=h1s[i % 2][:],
                                             in1=wbc[:, e * TC:(e + 1) * TC],
                                             op=ALU.mult))
            ve_m1e = c_ve.n

            if DBG_FFN and tag == "F1" and tc == 0:
                dlg = ph.enter_context(nc.sbuf_tensor("dlg" + tag, [128, 16], F32))

                @sec(V)
                def _(v):
                    v.wait_ge(c_pe.h, pe_g)
                    c_ve.inc(v.tensor_copy(out=dlg[:], in_=pf[2][:, 0:16]))
                dlg_ve = c_ve.n

                @sec(G)
                def _(g):
                    g.wait_ge(c_ve.h, ve_xln)
                    c_lg.inc(g.dma_start(out=dbg_xln2, in_=xln[:, 0:1024]), 16)
                    g.wait_ge(c_lg.h, c_lg.n)
                    g.wait_ge(c_ve.h, ve_m1e)
                    c_lg.inc(g.dma_start(out=dbg_xln, in_=xln[:, 0:1024]), 16)
                    c_lg.inc(g.dma_start(out=dbg_wbc, in_=wbc[:, 0:2048]), 16)
                    c_lg.inc(g.dma_start(out=dbg_h1w, in_=h1w[:, 0:1024]), 16)
                    c_lg.inc(g.dma_start(out=dbg_eL[:, 0:16], in_=eL[:]), 16)
                    c_lg.inc(g.dma_start(out=dbg_eL[:, 16:32], in_=wfT32[:, 0:16]), 16)
                    g.wait_ge(c_ve.h, dlg_ve)
                    c_lg.inc(g.dma_start(out=dbg_lg, in_=dlg[:]), 16)
                    c_lg.inc(g.dma_start(out=dbg_nmc[:, 0:4], in_=nmc[:]), 16)
                    c_lg.inc(g.dma_start(out=dbg_nmc[:, 4:12], in_=t8[:]), 16)
                    g.wait_ge(c_lg.h, c_lg.n)

                @sec(Y)
                def _(sy):
                    sy.wait_ge(c_ve.h, ve_m1e)
                    c_st.inc(sy.dma_start(out=dbg_stat, in_=stat[:]), 16)
                    sy.wait_ge(c_st.h, c_st.n)

            pe_m2 = c_pe.n
            ve_m2 = c_ve.n
            xfo_tot = []
            lw2tot = []
            sw2_0 = c_sw2.n

            @sec(G)
            def _(g, cc0=cc0):
                for ot in range(NKT):
                    if ot % 4 == 0:
                        bq = ot // 4
                        if bq >= 1:
                            g.wait_ge(c_ve.h, ve_m2 + 4 * bq)
                        g.wait_ge(c_ldx.h, c_ldx.n)
                        for k in range(4):
                            oo = bq * 4 + k
                            c_ldx.inc(g.dma_start(
                                out=x_fo[:, (oo % 4) * TC:(oo % 4 + 1) * TC],
                                in_=src[oo * 128:(oo + 1) * 128, cc0:cc0 + TC]), 16)
                        xfo_tot.append(c_ldx.n)
                    par = c_lga if ot % 2 == 0 else c_lgb
                    if ot >= 2:
                        g.wait_ge(c_pe.h, pe_m2 + ot - 1)
                        if cache_wr:
                            g.wait_ge(c_sw2.h, sw2_0 + 16 * (ot - 1))
                    else:
                        g.wait_ge(c_pe.h, w2rel[ot % 2])
                    g.wait_ge(par.h, par.n)
                    if cache_rd:
                        par.inc(g.dma_start(
                            out=w2b[ot % 2][:],
                            in_=w2c[:, ot * 8192:(ot + 1) * 8192]), 16)
                    else:
                        for e in range(NE):
                            srcw = w2_ext[e][:, ot * 128:(ot + 1) * 128]
                            srcw = srcw.rearrange("(kt p) c -> p kt c", p=128)
                            par.inc(g.dma_start(
                                out=w2b[ot % 2][:, e * NKT * 128:(e + 1) * NKT * 128],
                                in_=srcw), 16)
                    lw2tot.append((par, par.n))

            if cache_wr:
                @sec(S)
                def _(s):
                    for ot in range(NKT):
                        s.wait_ge(lw2tot[ot][0].h, lw2tot[ot][1])
                        c_sw2.inc(s.dma_start(
                            out=w2c[:, ot * 8192:(ot + 1) * 8192],
                            in_=w2b[ot % 2][:]), 16)

            @sec(T)
            def _(t):
                t.wait_ge(c_ve.h, ve_m1e)
                for ot in range(NKT):
                    t.wait_ge(lw2tot[ot][0].h, lw2tot[ot][1])
                    if ot >= 2:
                        t.wait_ge(c_ve.h, ve_m2 + ot - 1)
                    bank = 2 + (ot % 2)
                    first = True
                    for e in range(NE):
                        for ht in range(NKT):
                            i = e * NKT + ht
                            t.matmul(pf[bank][:],
                                     w2b[ot % 2][:, i * 128:(i + 1) * 128],
                                     h1w[:, i * TC:(i + 1) * TC],
                                     start=first, stop=False)
                            first = False
                    t.matmul(pf[bank][:],
                             b2_b[0:4, ot * 128:(ot + 1) * 128],
                             wrow[:],
                             start=False, stop=True, tile_position=TP0)
                    c_pe.inc(t.last)
                    w2rel[ot % 2] = c_pe.n

            sta0 = c_sta.n
            stb0 = c_stb.n

            @sec(V)
            def _(v):
                for ot in range(NKT):
                    v.wait_ge(c_pe.h, pe_m2 + ot + 1)
                    v.wait_ge(c_ldx.h, xfo_tot[ot // 4])
                    if ot >= 2:
                        spar, sbase = (c_sta, sta0) if ot % 2 == 0 else (c_stb, stb0)
                        v.wait_ge(spar.h, sbase + 16 * (ot // 2))
                    bank = 2 + (ot % 2)
                    c_ve.inc(v.tensor_tensor(
                        out=outs2[:, (ot % 2) * TC:(ot % 2 + 1) * TC],
                        in0=pf[bank][:], in1=x_fo[:, (ot % 4) * TC:(ot % 4 + 1) * TC],
                        op=ALU.add))
            ve_m2e = c_ve.n

            @sec(Y)
            def _(sy, cc0=cc0):
                for ot in range(NKT):
                    spar = c_sta if ot % 2 == 0 else c_stb
                    sy.wait_ge(c_ve.h, ve_m2 + ot + 1)
                    sy.wait_ge(spar.h, spar.n)
                    spar.inc(sy.dma_start(
                        out=dst[ot * 128:(ot + 1) * 128, cc0:cc0 + TC],
                        in_=outs2[:, (ot % 2) * TC:(ot % 2 + 1) * TC]), 16)
            x_rel = ve_m2e

        @sec(Y)
        def _(sy):
            sy.wait_ge(c_sta.h, c_sta.n)
            sy.wait_ge(c_stb.h, c_stb.n)
            sy.wait_ge(c_ldx.h, c_ldx.n)
            sy.wait_ge(c_stw.h, c_stw.n)
            sy.wait_ge(c_sw2.h, c_sw2.n)
        barrier()
        ph.close()

    if "ffn1" in PHASES:
        emit_ffn(u_ext, x_dram, "F1")
    else:
        @sec(Y)
        def _(sy):
            c_st.inc(sy.dma_start(out=x_dram, in_=u_ext), 16)
            sy.wait_ge(c_st.h, c_st.n)
        barrier()

    scan_es = contextlib.ExitStack()
    dtv_all = scan_es.enter_context(nc.sbuf_tensor("dtv_all", [64, RT], F32))
    S_run = scan_es.enter_context(nc.sbuf_tensor("S_run", [128, DI], F32))
    cstbS = scan_es.enter_context(nc.sbuf_tensor("cstbS", [128, 8192], BF16))
    ident_b = cstbS[:, 1024:1152]
    cbmask_b = cstbS[:, 1280:1792]
    ones64_b = cstbS[0:64, 1792:1920]
    rep64_b = cstbS[0:64, 2048:6144]

    @sec(Y)
    def _(sy):
        c_ld.inc(sy.dma_start(out=cstbS[:], in_=cst_b_ext), 16)
        sy.wait_ge(c_ld.h, c_ld.n)
    barrier()

    if "halo" in PHASES:
        hstage = scan_es.enter_context(nc.sbuf_tensor("hstage", [128, 48], F32))
        hstb = scan_es.enter_context(nc.sbuf_tensor("hstb", [128, 96], BF16))
        hbc = scan_es.enter_context(nc.sbuf_tensor("hbc", [128, 16], F32))

        @sec(G)
        def _(g):
            g.wait_ge(c_lg.h, c_lg.n)
            c_lg.inc(g.dma_start(
                out=hstage[:],
                in_=x_dram[:, RT - 3:RT].rearrange("(kt p) t -> p kt t", p=128)), 16)
            g.wait_ge(c_lg.h, c_lg.n)
            c_lg.inc(g.dma_start(out=hcc_in.ap(), in_=hstage[:]), 16)
            g.wait_ge(c_lg.h, c_lg.n)
            c_cc1.inc(g.collective_compute(
                "AllGather", ALU.bypass,
                replica_groups=[[0, 1], [2, 3], [4, 5], [6, 7]],
                ins=[hcc_in.ap().opt()], outs=[hcc_out.ap().opt()]), 1)
            g.wait_ge(c_cc1.h, c_cc1.n)
            c_lg.inc(g.dma_start(out=xhalo[:], in_=hcc_out.ap()[0:128, :]), 16)
        lg_h = c_lg.n

        @sec(V)
        def _(v):
            v.wait_ge(c_lg.h, lg_h)
            c_ve.inc(v.tensor_scalar(out=hstb[:, 0:48], in0=xhalo[:], scalar1=smask,
                                     scalar2=None, op0=ALU.mult))
        ve_h = c_ve.n

        @sec(S)
        def _(s):
            s.wait_ge(c_ve.h, ve_h)
            c_se.inc(s.activation(out=hstb[:, 48:96], in_=hstb[:, 0:48],
                                  func=AF.Square))
        se_h = c_se.n

        @sec(T)
        def _(t):
            t.wait_ge(c_se.h, se_h)
            for kt in range(NKT):
                t.matmul(pf[0][:, 0:3], onesS_b, hstb[:, kt * 3:(kt + 1) * 3],
                         start=(kt == 0), stop=(kt == NKT - 1))
            for kt in range(NKT):
                t.matmul(pf[1][:, 0:3], onesS_b, hstb[:, 48 + kt * 3:48 + (kt + 1) * 3],
                         start=(kt == 0), stop=(kt == NKT - 1))
            c_pe.inc(t.last)
        pe_h = c_pe.n

        @sec(V)
        def _(v):
            v.wait_ge(c_pe.h, pe_h)
            c_ve.inc(v.tensor_copy(out=hbc[:, 0:3], in_=pf[0][:, 0:3]))
            v.wait_ge(c_ve.h, c_ve.n)
            c_ve.inc(v.scalar_tensor_tensor(out=hbc[:, 4:7], in0=hbc[:, 0:3],
                                            scalar=-1.0, in1=hbc[:, 0:3],
                                            op0=ALU.mult, op1=ALU.mult))
            v.wait_ge(c_ve.h, c_ve.n)
            c_ve.inc(v.tensor_tensor(out=hbc[:, 4:7], in0=hbc[:, 4:7],
                                     in1=pf[1][:, 0:3], op=ALU.add))
        ve_hv = c_ve.n

        @sec(S)
        def _(s):
            s.wait_ge(c_ve.h, ve_hv)
            c_se.inc(s.activation(out=hbc[:, 8:11], in_=hbc[:, 4:7], func=AF.Ln,
                                  bias=eps_col))
            s.wait_ge(c_se.h, c_se.n)
            c_se.inc(s.activation(out=hbc[:, 4:7], in_=hbc[:, 8:11],
                                  func=AF.Exp, scale=-0.5))
        se_hr = c_se.n

        @sec(V)
        def _(v):
            v.wait_ge(c_se.h, se_hr)
            for kt in range(NKT):
                v.tensor_tensor(out=xln2h[:, kt * 3:(kt + 1) * 3],
                                in0=hstb[:, kt * 3:(kt + 1) * 3],
                                in1=hbc[:, 0:3], op=ALU.subtract)
            c_ve.inc(v.last)
            v.wait_ge(c_ve.h, c_ve.n)
            for kt in range(NKT):
                v.tensor_tensor(out=xln2h[:, kt * 3:(kt + 1) * 3],
                                in0=xln2h[:, kt * 3:(kt + 1) * 3],
                                in1=hbc[:, 4:7], op=ALU.mult)
            c_ve.inc(v.last)
        barrier()
    else:
        @sec(V)
        def _(v):
            v.memset(xln2h[:], 0)
            c_ve.inc(v.last)
        barrier()

    mam = contextlib.ExitStack()
    xln2 = mam.enter_context(nc.sbuf_tensor("xln2_all", [128, NKT * RT], BF16))
    if "ln2" in PHASES:
        ln2s = contextlib.ExitStack()
        psb = lambda n, s, d=F32: ln2s.enter_context(nc.sbuf_tensor(n, s, d))
        x_b2 = psb("ln2xb", [128, NKT * TC], BF16)
        xsq2 = psb("ln2sq", [128, NKT * TC], BF16)
        stat2 = psb("ln2st", [128, 1536])
        MUB = stat2[:, 0:512]
        BB = stat2[:, 512:1024]
        CC = stat2[:, 1024:1536]
        ln2_rel = 0
        for tc in range(NTC):
            cc0 = tc * TC

            @sec(G)
            def _(g, cc0=cc0, lr=ln2_rel):
                g.wait_ge(c_ve.h, lr)
                g.wait_ge(c_lg.h, c_lg.n)
                c_lg.inc(g.dma_start(
                    out=x_b2[:],
                    in_=x_dram[:, cc0:cc0 + TC].rearrange("(kt p) t -> p kt t", p=128)), 16)
            ldxb_r = c_lg.n

            @sec(S)
            def _(s, lr=ln2_rel):
                s.wait_ge(c_lg.h, ldxb_r)
                s.wait_ge(c_ve.h, lr)
                c_se.inc(s.activation(out=xsq2[:], in_=x_b2[:], func=AF.Square))
            se_sq = c_se.n

            @sec(T)
            def _(t, lr=ln2_rel):
                t.wait_ge(c_lg.h, ldxb_r)
                t.wait_ge(c_se.h, se_sq)
                t.wait_ge(c_ve.h, lr)
                for kt in range(NKT):
                    t.matmul(pf[0][:], onesS_b, x_b2[:, kt * TC:(kt + 1) * TC],
                             start=(kt == 0), stop=(kt == NKT - 1))
                for kt in range(NKT):
                    t.matmul(pf[1][:], onesS_b, xsq2[:, kt * TC:(kt + 1) * TC],
                             start=(kt == 0), stop=(kt == NKT - 1))
                c_pe.inc(t.last)
            pe_st = c_pe.n

            @sec(V)
            def _(v):
                v.wait_ge(c_pe.h, pe_st)
                v.tensor_copy(out=MUB, in_=pf[0][:])
                v.scalar_tensor_tensor(out=BB, in0=MUB, scalar=-1.0, in1=MUB,
                                       op0=ALU.mult, op1=ALU.mult)
                c_ve.inc(v.tensor_tensor(out=BB, in0=BB, in1=pf[1][:], op=ALU.add))
            ve_var = c_ve.n

            @sec(S)
            def _(s):
                s.wait_ge(c_ve.h, ve_var)
                s.activation(out=CC, in_=BB, func=AF.Ln, bias=eps_col)
                c_se.inc(s.activation(out=BB, in_=CC, func=AF.Exp, scale=-0.5))
            se_rstd = c_se.n

            @sec(V)
            def _(v):
                v.wait_ge(c_se.h, se_rstd)
                c_ve.inc(v.scalar_tensor_tensor(out=CC, in0=MUB, scalar=-1.0,
                                                in1=BB, op0=ALU.mult, op1=ALU.mult))
            ve_c0 = c_ve.n

            @sec(V)
            def _(v, cc0=cc0):
                v.wait_ge(c_ve.h, ve_c0)
                for kt in range(NKT):
                    o = kt * RT + cc0
                    v.tensor_tensor(out=xln2[:, o:o + TC],
                                    in0=x_b2[:, kt * TC:(kt + 1) * TC],
                                    in1=BB, op=ALU.mult)
                for kt in range(NKT):
                    o = kt * RT + cc0
                    v.tensor_add(xln2[:, o:o + TC], xln2[:, o:o + TC], CC)
                c_ve.inc(v.last)
            ln2_rel = c_ve.n
        barrier()
        ln2s.close()

    if "inproj" in PHASES:
        ips = contextlib.ExitStack()
        psb = lambda n, s, d=F32: ips.enter_context(nc.sbuf_tensor(n, s, d))
        wipb = [psb(f"wipb{i}", [128, NKT * 128], BF16) for i in range(2)]
        strip = psb("strip", [128, 515], BF16)
        accb = [psb(f"ipacc{i}", [128, TC], BF16) for i in range(2)]
        stg = [psb(f"ipst{i}", [128, TC], BF16) for i in range(2)]
        dts = psb("dts", [64, 4 * TC])
        NOC = 67
        pe0 = c_pe.n
        evA0 = c_evA.n
        se0_ = c_se.n
        st0a = c_sta.n
        st0b = c_stb.n
        ve_start = c_ve.n

        def is_xbc(oc):
            return 32 <= oc < 66

        def pe_at(oc, tc):
            v = pe0 + sum(NTC + (1 if is_xbc(o) else 0) for o in range(oc))
            if is_xbc(oc):
                v += 1
            return v + tc + 1

        def pe_halo_at(oc):
            return pe0 + sum(NTC + (1 if is_xbc(o) else 0) for o in range(oc)) + 1

        def evA_blk(oc):
            v = evA0
            for o in range(oc):
                v += (1 + 2 * NTC) if is_xbc(o) else NTC
            return v

        def evA_evac(oc, tc):
            if is_xbc(oc):
                return evA_blk(oc) + 1 + 2 * tc + 1
            return evA_blk(oc) + tc + 1

        def evA_conv(oc, tc):
            return evA_blk(oc) + 1 + 2 * tc + 2

        def evA_halo(oc):
            return evA_blk(oc) + 1

        def se_silu(oc, tc):
            n = sum(NTC for o in range(32, oc) if is_xbc(o))
            return se0_ + n + tc + 1

        def st_par(j):
            return (c_sta, st0a) if j % 2 == 0 else (c_stb, st0b)

        def st_done(j):
            sem, base = st_par(j)
            return sem, base + 16 * (j // 2 + 1)

        def ip_src_col(oc):
            return 8384 if oc == NOC - 1 else oc * 128

        lgiptot = []

        @sec(G)
        def _(g):
            for oc in range(NOC):
                par = c_lga if oc % 2 == 0 else c_lgb
                if oc >= 2:
                    g.wait_ge(c_pe.h, pe_at(oc - 2, NTC - 1))
                g.wait_ge(par.h, par.n)
                srcw = w_ip[:, ip_src_col(oc):ip_src_col(oc) + 128]
                srcw = srcw.rearrange("(kt p) c -> p kt c", p=128)
                par.inc(g.dma_start(out=wipb[oc % 2][:], in_=srcw), 16)
                lgiptot.append((par, par.n))

        @sec(T)
        def _(t):
            t.wait_ge(c_ve.h, ve_start)
            for oc in range(NOC):
                t.wait_ge(lgiptot[oc][0].h, lgiptot[oc][1])
                if is_xbc(oc):
                    if oc >= 33:
                        t.wait_ge(c_evA.h, evA_halo(oc - 1))
                    for kt in range(NKT):
                        t.matmul(pf[4][:, 0:3], wipb[oc % 2][:, kt * 128:(kt + 1) * 128],
                                 xln2h[:, kt * 3:(kt + 1) * 3],
                                 start=(kt == 0), stop=(kt == NKT - 1))
                    c_pe.inc(t.last)
                for tc in range(NTC):
                    j = oc * NTC + tc
                    if j >= 2:
                        jo, jt = divmod(j - 2, NTC)
                        t.wait_ge(c_evA.h, evA_evac(jo, jt))
                    for kt in range(NKT):
                        t.matmul(pf[tc % 2][:],
                                 wipb[oc % 2][:, kt * 128:(kt + 1) * 128],
                                 xln2[:, kt * RT + tc * TC:
                                      kt * RT + tc * TC + TC],
                                 start=(kt == 0), stop=(kt == NKT - 1))
                    c_pe.inc(t.last)
        assert c_pe.n == pe_at(NOC - 1, NTC - 1), (c_pe.n, pe_at(NOC - 1, NTC - 1))

        @sec(V)
        def _(v):
            for oc in range(NOC):
                if oc < 32:
                    for tc in range(NTC):
                        v.wait_ge(c_pe.h, pe_at(oc, tc))
                        j = oc * NTC + tc
                        if j >= 2:
                            sem, val = st_done(j - 2)
                            v.wait_ge(sem.h, val)
                        v.tensor_copy(out=stg[tc % 2][:], in_=pf[tc % 2][:])
                        c_evA.inc(v.last)
                elif is_xbc(oc):
                    ti = oc - 32
                    v.wait_ge(c_pe.h, pe_halo_at(oc))
                    v.tensor_copy(out=strip[:, 0:3], in_=pf[4][:, 0:3])
                    c_evA.inc(v.last)
                    v.wait_ge(c_evA.h, c_evA.n)
                    for tc in range(NTC):
                        v.wait_ge(c_pe.h, pe_at(oc, tc))
                        v.tensor_copy(out=strip[:, 3:515], in_=pf[tc % 2][:])
                        c_evA.inc(v.last)
                        v.wait_ge(c_evA.h, c_evA.n)
                        j = oc * NTC + tc
                        if j >= 2:
                            sem, val = st_done(j - 2)
                            v.wait_ge(sem.h, val)
                        v.tensor_scalar(out=accb[tc % 2][:], in0=strip[:, 0:512],
                                        scalar1=convw(ti, 0), scalar2=None,
                                        op0=ALU.mult)
                        for k in range(1, 4):
                            v.scalar_tensor_tensor(out=accb[tc % 2][:],
                                                   in0=strip[:, k:512 + k],
                                                   scalar=convw(ti, k),
                                                   in1=accb[tc % 2][:],
                                                   op0=ALU.mult, op1=ALU.add)
                        v.tensor_copy(out=strip[:, 0:3], in_=strip[:, 512:515])
                        c_evA.inc(v.last)
                        v.wait_ge(c_evA.h, c_evA.n)
                else:
                    for tc in range(NTC):
                        v.wait_ge(c_pe.h, pe_at(oc, tc))
                        v.tensor_scalar(out=dts[:, 0:TC], in0=pf[tc % 2][64:128, :],
                                        scalar1=dtb_col, scalar2=None, op0=ALU.add)
                        c_evA.inc(v.last)
                        v.wait_ge(c_se.h, se_silu(66, 0) + tc)
                        c_ve.inc(v.tensor_add(dtv_all[:, tc * TC:(tc + 1) * TC],
                                              dts[:, TC:2 * TC], dts[:, 2 * TC:3 * TC]))
        assert c_evA.n == evA_blk(66) + NTC, (c_evA.n, evA_blk(66) + NTC)

        @sec(S)
        def _(s):
            for oc in range(32, 66):
                ti = oc - 32
                for tc in range(NTC):
                    s.wait_ge(c_evA.h, evA_conv(oc, tc))
                    j = oc * NTC + tc
                    if j >= 2:
                        sem, val = st_done(j - 2)
                        s.wait_ge(sem.h, val)
                    s.activation(out=stg[tc % 2][:], in_=accb[tc % 2][:],
                                 func=AF.Silu, bias=convb(ti))
                    c_se.inc(s.last)
            for tc in range(NTC):
                s.wait_ge(c_evA.h, evA_evac(66, tc))
                s.activation(out=dts[:, TC:2 * TC], in_=dts[:, 0:TC], func=AF.Relu)
                s.activation(out=dts[:, 2 * TC:3 * TC], in_=dts[:, 0:TC], func=AF.Abs)
                s.activation(out=dts[:, 3 * TC:4 * TC], in_=dts[:, 2 * TC:3 * TC],
                             func=AF.Exp, scale=-1.0)
                s.activation(out=dts[:, 2 * TC:3 * TC], in_=dts[:, 3 * TC:4 * TC],
                             func=AF.Ln, bias=one_col64)
                c_se.inc(s.last)
        assert c_se.n == se0_ + 136 + NTC, (c_se.n, se0_ + 136 + NTC)

        @sec(Y)
        def _(sy):
            for oc in range(66):
                for tc in range(NTC):
                    j = oc * NTC + tc
                    sem, base = st_par(j)
                    if oc < 32:
                        sy.wait_ge(c_evA.h, evA_evac(oc, tc))
                        dstp = z_dram[oc * 128:(oc + 1) * 128, tc * TC:(tc + 1) * TC]
                    else:
                        sy.wait_ge(c_se.h, se_silu(oc, tc))
                        ti = oc - 32
                        dstp = xcb_dram[ti * 128:(ti + 1) * 128, tc * TC:(tc + 1) * TC]
                    sy.wait_ge(sem.h, sem.n)
                    sem.inc(sy.dma_start(out=dstp, in_=stg[tc % 2][:]), 16)
        assert c_sta.n == st0a + 16 * 132 and c_stb.n == st0b + 16 * 132

        @sec(Y)
        def _(sy):
            sy.wait_ge(c_sta.h, c_sta.n)
            sy.wait_ge(c_stb.h, c_stb.n)
        barrier()
        ips.close()
    mam.close()

    if "s1" in PHASES:
        s1s = contextlib.ExitStack()
        psb = lambda n, s, d=F32: s1s.enter_context(nc.sbuf_tensor(n, s, d))
        xch = psb("s1xch", [128, 32 * CHK], BF16)
        Bt_in = psb("s1B", [128, CHK], BF16)
        xT_sb = psb("s1xT", [128, 2 * DI], BF16)
        xds = psb("s1xds", [128, 2 * DI], BF16)
        BT_sb = psb("s1BT", [128, 256], BF16)
        ones_t = psb("s1ones", [64, CHK])
        ac_t = psb("s1ac", [64, CHK])
        acum_t = psb("s1acum", [128, CHK])
        dec_t = psb("s1dec", [64, CHK])
        dd_t = psb("s1dd", [128, CHK])
        ddT = psb("s1ddT", [128, 128])
        cdc = psb("s1cdc", [64, 1])
        cdb64a = psb("s1cdb", [64, 128], BF16)
        states_sb = psb("s1st", [128, DI])

        @sec(V)
        def _(v):
            v.memset(ones_t[:], 1.0)
            v.memset(S_run[:], 0.0)
            v.memset(dd_t[:], 0.0)
            v.memset(acum_t[:], 0.0)
            c_ve.inc(v.last)
        barrier()

        for c in range(NCHK):
            q0 = c * CHK

            @sec(Y)
            def _(sy, q0=q0):
                sy.wait_ge(c_ld.h, c_ld.n)
                c_ld.inc(sy.dma_start(
                    out=xch[:],
                    in_=xcb_dram[0:DI, q0:q0 + CHK].rearrange("(t p) q -> p t q", p=128)), 16)
                c_ld.inc(sy.dma_start(out=Bt_in[:],
                                      in_=xcb_dram[DI:DI + 128, q0:q0 + CHK]), 16)
            ld_c = c_ld.n

            @sec(V)
            def _(v, c=c):
                c_ve.inc(v.tensor_scalar(out=ac_t[:], in0=dtv_all[:, c * CHK:(c + 1) * CHK],
                                         scalar1=acol[:], scalar2=None, op0=ALU.mult))
                v.wait_ge(c_ve.h, c_ve.n)
                c_ve.inc(v.tensor_tensor_scan(out=acum_t[0:64, :], data0=ones_t[:],
                                              data1=ac_t[:], initial=0.0,
                                              op0=ALU.mult, op1=ALU.add))
            ve_ac = c_ve.n

            @sec(S)
            def _(s):
                s.wait_ge(c_ve.h, ve_ac)
                s.activation(out=cdc[:], in_=acum_t[0:64, CHK - 1:CHK], func=AF.Exp)
                c_se.inc(s.activation(out=dec_t[:], in_=acum_t[0:64, :], func=AF.Exp,
                                      scale=-1.0, bias=acum_t[0:64, CHK - 1:CHK]))
            se_dec = c_se.n

            @sec(V)
            def _(v, c=c):
                v.wait_ge(c_se.h, se_dec)
                c_ve.inc(v.tensor_tensor(out=dd_t[0:64, :], in0=dec_t[:],
                                         in1=dtv_all[:, c * CHK:(c + 1) * CHK],
                                         op=ALU.mult))
            ve_dd = c_ve.n

            @sec(T)
            def _(t):
                t.wait_ge(c_ve.h, ve_dd)
                for sg in range(2):
                    t.transpose(pf[0][:, sg * 128:(sg + 1) * 128],
                                dd_t[:, sg * 128:(sg + 1) * 128], ident_f)
                c_pe.inc(t.last)
            pe_ddT = c_pe.n

            @sec(V)
            def _(v):
                v.wait_ge(c_pe.h, pe_ddT)
                v.tensor_copy(out=ddT[:, 0:64], in_=pf[0][:, 0:64])
                c_ve.inc(v.tensor_copy(out=ddT[:, 64:128], in_=pf[0][:, 128:192]))
            ve_ddT = c_ve.n

            pe_xt0 = c_pe.n
            ve_xt0 = c_ve.n

            @sec(T)
            def _(t):
                t.wait_ge(c_ld.h, ld_c)
                gidx = 0
                for sg in range(2):
                    for pt in range(32):
                        bkt = gidx % 2
                        if gidx >= 2 and pt % 8 == 0:
                            t.wait_ge(c_ve.h, ve_xt0 + gidx - 1)
                        t.transpose(pb[bkt][:, (pt % 8) * 128:(pt % 8 + 1) * 128],
                                    xch[:, pt * CHK + sg * 128:pt * CHK + (sg + 1) * 128],
                                    ident_b)
                        if pt % 8 == 7:
                            c_pe.inc(t.last)
                            gidx += 1
                for sg in range(2):
                    bkt = gidx % 2
                    t.wait_ge(c_ve.h, ve_xt0 + gidx - 1)
                    t.transpose(pb[bkt][:, 0:128], Bt_in[:, sg * 128:(sg + 1) * 128],
                                ident_b)
                    c_pe.inc(t.last)
                    gidx += 1

            @sec(V)
            def _(v):
                for gidx in range(10):
                    v.wait_ge(c_pe.h, pe_xt0 + gidx + 1)
                    bkt = gidx % 2
                    if gidx < 8:
                        sg, blk = divmod(gidx, 4)
                        dst = xT_sb[:, sg * DI + blk * 1024:sg * DI + (blk + 1) * 1024]
                        c_ve.inc(v.tensor_copy(out=dst, in_=pb[bkt][:]))
                    else:
                        sg = gidx - 8
                        c_ve.inc(v.tensor_copy(out=BT_sb[:, sg * 128:(sg + 1) * 128],
                                               in_=pb[bkt][:, 0:128]))
            ve_xtE = c_ve.n

            @sec(V)
            def _(v):
                v.wait_ge(c_ve.h, ve_xtE)
                for sg in range(2):
                    for h in range(NH):
                        v.tensor_scalar(out=xds[:, sg * DI + h * 64:sg * DI + (h + 1) * 64],
                                        in0=xT_sb[:, sg * DI + h * 64:sg * DI + (h + 1) * 64],
                                        scalar1=ddT[:, sg * 64 + h:sg * 64 + h + 1],
                                        scalar2=None, op0=ALU.mult)
                c_ve.inc(v.last)
            ve_xds = c_ve.n

            pe_sm0 = c_pe.n
            ve_sm0 = c_ve.n

            @sec(T)
            def _(t):
                t.wait_ge(c_ve.h, ve_xds)
                for hg in range(8):
                    if hg >= 2:
                        t.wait_ge(c_ve.h, ve_sm0 + hg - 1)
                    bank = 1 + (hg % 2)
                    for sg in range(2):
                        t.matmul(pf[bank][:],
                                 BT_sb[:, sg * 128:(sg + 1) * 128],
                                 xds[:, sg * DI + hg * 512:
                                      sg * DI + (hg + 1) * 512],
                                 start=(sg == 0), stop=(sg == 1))
                    c_pe.inc(t.last)

            @sec(V)
            def _(v):
                for hg in range(8):
                    v.wait_ge(c_pe.h, pe_sm0 + hg + 1)
                    bank = 1 + (hg % 2)
                    c_ve.inc(v.tensor_copy(out=states_sb[:, hg * 512:(hg + 1) * 512],
                                           in_=pf[bank][:]))
            ve_smE = c_ve.n

            @sec(Y)
            def _(sy, c=c):
                sy.wait_ge(c_ve.h, ve_xtE)
                sy.wait_ge(c_st.h, c_st.n)
                for sg in range(2):
                    c_st.inc(sy.dma_start(
                        out=xt_dram[c * CHK + sg * 128:c * CHK + (sg + 1) * 128, :],
                        in_=xT_sb[:, sg * DI:(sg + 1) * DI]), 16)
                sy.wait_ge(c_st.h, c_st.n)

            @sec(G)
            def _(g, c=c):
                g.wait_ge(c_ve.h, ve_smE)
                g.wait_ge(c_lg.h, c_lg.n)
                c_lg.inc(g.dma_start(
                    out=st_dram[c * 128:(c + 1) * 128, :], in_=states_sb[:],
                    max_dma_last_dim=4096), 16)
                g.wait_ge(c_lg.h, c_lg.n)

            @sec(V)
            def _(v):
                c_ve.inc(v.tensor_scalar(out=cdb64a[:], in0=ones64_b,
                                         scalar1=cdc[:], scalar2=None, op0=ALU.mult))
            ve_rep = c_ve.n

            pe_cd0 = c_pe.n
            ve_cd0 = c_ve.n

            @sec(T)
            def _(t):
                t.wait_ge(c_ve.h, ve_rep)
                for gq in range(8):
                    if gq >= 2:
                        t.wait_ge(c_ve.h, ve_cd0 + 2 * (gq - 1))
                    bank = 3 + (gq % 2)
                    t.matmul(pf[bank][:], cdb64a[:],
                             rep64_b[:, gq * 512:(gq + 1) * 512],
                             start=True, stop=True, tile_position=TP0)
                    c_pe.inc(t.last)

            @sec(V)
            def _(v):
                for gq in range(8):
                    v.wait_ge(c_pe.h, pe_cd0 + gq + 1)
                    bank = 3 + (gq % 2)
                    v.tensor_tensor(out=S_run[:, gq * 512:(gq + 1) * 512],
                                    in0=S_run[:, gq * 512:(gq + 1) * 512],
                                    in1=pf[bank][:], op=ALU.mult)
                    c_ve.inc(v.last)
                    v.tensor_add(S_run[:, gq * 512:(gq + 1) * 512],
                                 S_run[:, gq * 512:(gq + 1) * 512],
                                 states_sb[:, gq * 512:(gq + 1) * 512])
                    c_ve.inc(v.last)
            barrier()

        @sec(G)
        def _(g):
            c_lg.inc(g.dma_start(out=scc_in.ap(), in_=S_run[:]), 16)
            g.wait_ge(c_lg.h, c_lg.n)
            c_cc2.inc(g.collective_compute(
                "AllGather", ALU.bypass,
                replica_groups=[[0, 1], [2, 3], [4, 5], [6, 7]],
                ins=[scc_in.ap().opt()], outs=[scc_out.ap().opt()]), 1)
            g.wait_ge(c_cc2.h, c_cc2.n)
            c_lg.inc(g.dma_start(out=states_sb[:], in_=scc_out.ap()[0:128, :]), 16)
        lg_s = c_lg.n

        @sec(V)
        def _(v):
            v.wait_ge(c_lg.h, lg_s)
            c_ve.inc(v.tensor_scalar(out=S_run[:], in0=states_sb[:], scalar1=smask,
                                     scalar2=None, op0=ALU.mult))
        barrier()
        s1s.close()

    if "s2" in PHASES:
        s2s = contextlib.ExitStack()
        psb = lambda n, s, d=F32: s2s.enter_context(nc.sbuf_tensor(n, s, d))
        ACq = psb("ACq", [33, 2 * DI], F32)
        EAq = psb("EAq", [33, 2 * DI], BF16)
        xT2 = psb("s2xT", [128, 2 * DI], BF16)
        xch2 = psb("s2xch", [128, 32 * CHK], BF16)
        z_sb = psb("s2z", [128, 32 * CHK], BF16)
        states2 = psb("s2st", [128, DI], BF16)
        prevbf = psb("s2prev", [128, DI], BF16)
        BC_sb = psb("s2BC", [128, 512], BF16)
        CBtm = psb("s2CBtm", [128, 512], BF16)
        G2 = psb("s2G", [128, 1024], BF16)
        expD2 = psb("s2eD", [128, 1024], BF16)
        Cs2 = psb("s2Cs", [128, 512], BF16)
        ones_t2 = psb("s2ones", [64, CHK])
        ac2 = psb("s2ac", [64, CHK])
        acum2 = psb("s2acum", [128, CHK])
        eAr = psb("s2eA", [64, CHK])
        negAT = psb("s2nAT", [128, 128])
        dtvT = psb("s2dtvT", [128, 128])
        dtvp = psb("s2dtvp", [128, CHK])
        cdc2 = psb("s2cdc", [64, 1])
        cdb64b = psb("s2cdb", [64, 128], BF16)
        yg = psb("s2yg", [128, 32 * CHK], BF16)
        ypair = psb("s2yp", [128, CHK])
        ysq = xT2
        yn_sb = xch2
        rmsbc = psb("s2rms", [128, 512])

        @sec(V)
        def _(v):
            v.memset(ones_t2[:], 1.0)
            v.memset(acum2[:], 0.0)
            v.memset(dtvp[:], 0.0)
            c_ve.inc(v.last)
        barrier()

        for c in range(NCHK):
            q0 = c * CHK

            @sec(Y)
            def _(sy, c=c, q0=q0):
                sy.wait_ge(c_ld.h, c_ld.n)
                c_ld.inc(sy.dma_start(
                    out=xT2[:],
                    in_=xt_dram[q0:q0 + CHK, :].rearrange("(sg p) n -> p sg n", p=128)), 16)
                c_ld.inc(sy.dma_start(
                    out=xch2[:],
                    in_=xcb_dram[0:DI, q0:q0 + CHK].rearrange("(t p) q -> p t q", p=128)), 16)
                c_ld.inc(sy.dma_start(
                    out=z_sb[:],
                    in_=z_dram[:, q0:q0 + CHK].rearrange("(t p) q -> p t q", p=128)), 16)
                c_ld.inc(sy.dma_start(out=states2[:],
                                      in_=st_dram[c * 128:(c + 1) * 128, :]), 16)
                c_ld.inc(sy.dma_start(out=BC_sb[:, 0:256],
                                      in_=xcb_dram[DI:DI + 128, q0:q0 + CHK]), 16)
                c_ld.inc(sy.dma_start(out=BC_sb[:, 256:512],
                                      in_=xcb_dram[DI + 128:DI + 256, q0:q0 + CHK]), 16)
            ld_c = c_ld.n

            @sec(V)
            def _(v, c=c):
                v.tensor_scalar(out=ac2[:], in0=dtv_all[:, c * CHK:(c + 1) * CHK],
                                scalar1=acol[:], scalar2=None, op0=ALU.mult)
                c_ve.inc(v.tensor_copy(out=dtvp[0:64, :],
                                       in_=dtv_all[:, c * CHK:(c + 1) * CHK]))
                v.wait_ge(c_ve.h, c_ve.n)
                c_ve.inc(v.tensor_tensor_scan(out=acum2[0:64, :], data0=ones_t2[:],
                                              data1=ac2[:], initial=0.0,
                                              op0=ALU.mult, op1=ALU.add))
            ve_ac = c_ve.n

            @sec(S)
            def _(s):
                s.wait_ge(c_ve.h, ve_ac)
                s.activation(out=eAr[:], in_=acum2[0:64, :], func=AF.Exp)
                c_se.inc(s.activation(out=cdc2[:], in_=acum2[0:64, CHK - 1:CHK],
                                      func=AF.Exp))
            se_ea = c_se.n

            @sec(Y)
            def _(sy):
                sy.wait_ge(c_se.h, se_ea)
                sy.wait_ge(c_st.h, c_st.n)
                c_st.inc(sy.dma_start(out=ac_dram, in_=acum2[0:64, :]), 16)
                c_st.inc(sy.dma_start(out=ea_dram, in_=eAr[:]), 16)
                sy.wait_ge(c_st.h, c_st.n)
                sy.wait_ge(c_ld.h, c_ld.n)
                for qd in range(4):
                    r0 = 32 * (qd & 1)
                    cb = DI * (qd >> 1)
                    c_ld.inc(sy.dma_start(
                        out=ACq[r0:r0 + 1, cb:cb + DI],
                        in_=ac_dram[16 * qd:16 * (qd + 1), :]), 16)
            ld_q = c_ld.n

            @sec(G)
            def _(g):
                g.wait_ge(c_st.h, c_st.n)
                g.wait_ge(c_lg.h, c_lg.n)
                for qd in range(4):
                    r0 = 32 * (qd & 1)
                    cb = DI * (qd >> 1)
                    c_lg.inc(g.dma_start(
                        out=EAq[r0:r0 + 1, cb:cb + DI],
                        in_=ea_dram[16 * qd:16 * (qd + 1), :]), 16)
            lg_q = c_lg.n

            @sec(T)
            def _(t):
                t.wait_ge(c_ve.h, ve_ac)
                for sg in range(2):
                    t.transpose(pf[5][:, sg * 128:(sg + 1) * 128],
                                acum2[:, sg * 128:(sg + 1) * 128], ident_f)
                c_pe.inc(t.last)
            pe_tp1 = c_pe.n

            @sec(V)
            def _(v):
                v.wait_ge(c_pe.h, pe_tp1)
                v.tensor_scalar(out=negAT[:, 0:64], in0=pf[5][:, 0:64],
                                scalar1=-1.0, scalar2=None, op0=ALU.mult)
                c_ve.inc(v.tensor_scalar(out=negAT[:, 64:128], in0=pf[5][:, 128:192],
                                         scalar1=-1.0, scalar2=None, op0=ALU.mult))
            ve_tp1 = c_ve.n

            @sec(T)
            def _(t):
                t.wait_ge(c_ve.h, ve_tp1)
                for sg in range(2):
                    t.transpose(pf[5][:, sg * 128:(sg + 1) * 128],
                                dtvp[:, sg * 128:(sg + 1) * 128], ident_f)
                c_pe.inc(t.last)
            pe_tp2 = c_pe.n

            @sec(V)
            def _(v):
                v.wait_ge(c_pe.h, pe_tp2)
                v.tensor_copy(out=dtvT[:, 0:64], in_=pf[5][:, 0:64])
                v.tensor_copy(out=dtvT[:, 64:128], in_=pf[5][:, 128:192])
                c_ve.inc(v.tensor_copy(out=prevbf[:], in_=S_run[:]))
            ve_tp = c_ve.n

            @sec(T)
            def _(t):
                t.wait_ge(c_ld.h, ld_c)
                t.matmul(pf[4][:, 0:256], BC_sb[:, 0:128], BC_sb[:, 256:512],
                         start=True, stop=False)
                t.matmul(pf[4][:, 256:512], BC_sb[:, 128:256], BC_sb[:, 256:512],
                         start=False, stop=True)
                c_pe.inc(t.last)
            pe_cbt = c_pe.n

            @sec(V)
            def _(v):
                v.wait_ge(c_pe.h, pe_cbt)
                c_ve.inc(v.tensor_tensor(out=CBtm[:], in0=pf[4][:], in1=cbmask_b,
                                         op=ALU.mult))
            ve_cbt = c_ve.n

            @sec(S)
            def _(s):
                s.wait_ge(c_ld.h, ld_c)
                c_se.inc(s.activation(out=yg[:], in_=z_sb[:], func=AF.Silu))
            se_siluz = c_se.n

            for j in range(32):
                h0, h1 = 2 * j, 2 * j + 1
                ve_j0 = c_ve.n

                @sec(T)
                def _(t, h0=h0, h1=h1, j=j, ve_j0=ve_j0):
                    t.wait_ge(c_ld.h, ld_q)
                    t.wait_ge(c_lg.h, lg_q)
                    t.wait_ge(c_ve.h, ve_cbt)
                    t.wait_ge(c_ve.h, ve_tp)
                    if j >= 1:
                        t.wait_ge(c_ve.h, ve_j0)
                    for idx, hh in enumerate((h0, h1)):
                        qd, qo = divmod(hh, 16)
                        r0 = 32 * (qd & 1)
                        cb = DI * (qd >> 1)
                        t.matmul(pf[0][:, idx * 256:(idx + 1) * 256], onesb4(qd),
                                 EAq[r0:r0 + 1, cb + qo * 256:cb + (qo + 1) * 256],
                                 start=(idx == 0), stop=(idx == 1),
                                 tile_position=(r0, 0))
                        t.matmul(pf[1 + idx][:, 0:256], ones4(qd),
                                 ACq[r0:r0 + 1, cb + qo * 256:cb + (qo + 1) * 256],
                                 start=True, stop=False)
                        t.matmul(pf[1 + idx][:, 256:512], ones4(qd),
                                 ACq[r0:r0 + 1, cb + qo * 256:cb + (qo + 1) * 256],
                                 start=False, stop=False)
                        t.matmul(pf[1 + idx][:], ident_f, dmask_f[:],
                                 start=False, stop=True)
                    c_pe.inc(t.last)
                pe_jD = c_pe.n

                @sec(S)
                def _(s, h0=h0, h1=h1, ve_j0=ve_j0, j=j):
                    s.wait_ge(c_pe.h, pe_jD)
                    if j >= 1:
                        s.wait_ge(c_ve.h, ve_j0)
                    for idx, hh in enumerate((h0, h1)):
                        for sg in range(2):
                            s.activation(
                                out=expD2[:, idx * 512 + sg * 256:idx * 512 + (sg + 1) * 256],
                                in_=pf[1 + idx][:, sg * 256:(sg + 1) * 256],
                                func=AF.Exp,
                                bias=negAT[:, sg * 64 + hh:sg * 64 + hh + 1])
                    c_se.inc(s.last)
                se_jE = c_se.n

                @sec(V)
                def _(v, h0=h0, h1=h1):
                    v.wait_ge(c_se.h, se_jE)
                    for idx, hh in enumerate((h0, h1)):
                        for sg in range(2):
                            v.scalar_tensor_tensor(
                                out=G2[:, idx * 512 + sg * 256:idx * 512 + (sg + 1) * 256],
                                in0=expD2[:, idx * 512 + sg * 256:idx * 512 + (sg + 1) * 256],
                                scalar=dtvT[:, sg * 64 + hh:sg * 64 + hh + 1],
                                in1=CBtm[:, sg * 256:(sg + 1) * 256],
                                op0=ALU.mult, op1=ALU.mult)
                        v.tensor_tensor(out=Cs2[:, idx * 256:(idx + 1) * 256],
                                        in0=BC_sb[:, 256:512],
                                        in1=pf[0][:, idx * 256:(idx + 1) * 256],
                                        op=ALU.mult)
                    c_ve.inc(v.last)
                ve_jG = c_ve.n

                @sec(T)
                def _(t, h0=h0, h1=h1):
                    t.wait_ge(c_ve.h, ve_jG)
                    for idx, hh in enumerate((h0, h1)):
                        base = idx * 64
                        tp = (0, base)
                        t.matmul(pf[4][base:base + 64, 0:256],
                                 xT2[:, hh * 64:(hh + 1) * 64],
                                 G2[:, idx * 512:idx * 512 + 256],
                                 start=True, stop=False, tile_position=tp)
                        t.matmul(pf[4][base:base + 64, 0:256],
                                 xT2[:, DI + hh * 64:DI + (hh + 1) * 64],
                                 G2[:, idx * 512 + 256:idx * 512 + 512],
                                 start=False, stop=False, tile_position=tp)
                        t.matmul(pf[4][base:base + 64, 0:256],
                                 prevbf[:, hh * 64:(hh + 1) * 64],
                                 Cs2[:, idx * 256:(idx + 1) * 256],
                                 start=False, stop=True, tile_position=tp)
                    c_pe.inc(t.last)
                pe_jY = c_pe.n

                @sec(V)
                def _(v, j=j):
                    v.wait_ge(c_pe.h, pe_jY)
                    if j == 0:
                        v.wait_ge(c_se.h, se_siluz)
                    c_ve.inc(v.scalar_tensor_tensor(out=ypair[:, 0:256],
                                                    in0=xch2[:, j * 256:(j + 1) * 256],
                                                    scalar=drep(j), in1=pf[4][:, 0:256],
                                                    op0=ALU.mult, op1=ALU.add))
                    v.wait_ge(c_ve.h, c_ve.n)
                    v.tensor_tensor(out=yg[:, j * 256:(j + 1) * 256],
                                    in0=ypair[:, 0:256],
                                    in1=yg[:, j * 256:(j + 1) * 256], op=ALU.mult)
                    c_ve.inc(v.last)
            ve_yall = c_ve.n

            @sec(S)
            def _(s):
                s.wait_ge(c_ve.h, ve_yall)
                c_se.inc(s.activation(out=ysq[:], in_=yg[:], func=AF.Square))
            se_ysq = c_se.n

            @sec(T)
            def _(t):
                t.wait_ge(c_se.h, se_ysq)
                for pt in range(32):
                    t.matmul(pf[0][:, 0:256], onesSi_b,
                             ysq[:, pt * 256:(pt + 1) * 256],
                             start=(pt == 0), stop=(pt == 31))
                c_pe.inc(t.last)
            pe_v = c_pe.n

            @sec(S)
            def _(s):
                s.wait_ge(c_pe.h, pe_v)
                c_se.inc(s.activation(out=rmsbc[:, 256:512], in_=pf[0][:, 0:256],
                                      func=AF.Ln, bias=eps_col))
                s.wait_ge(c_se.h, c_se.n)
                c_se.inc(s.activation(out=rmsbc[:, 0:256], in_=rmsbc[:, 256:512],
                                      func=AF.Exp, scale=-0.5))
            se_rs = c_se.n

            @sec(V)
            def _(v):
                v.wait_ge(c_se.h, se_rs)
                for pt in range(32):
                    v.scalar_tensor_tensor(out=yn_sb[:, pt * 256:(pt + 1) * 256],
                                           in0=yg[:, pt * 256:(pt + 1) * 256],
                                           scalar=nwcol(pt), in1=rmsbc[:, 0:256],
                                           op0=ALU.mult, op1=ALU.mult)
                c_ve.inc(v.last)
            ve_yn = c_ve.n

            @sec(Y)
            def _(sy, q0=q0):
                sy.wait_ge(c_ve.h, ve_yn)
                sy.wait_ge(c_st.h, c_st.n)
                c_st.inc(sy.dma_start(
                    out=yn_dram[:, q0:q0 + CHK].rearrange("(t p) q -> p t q", p=128),
                    in_=yn_sb[:]), 16)
                sy.wait_ge(c_st.h, c_st.n)

            @sec(V)
            def _(v):
                c_ve.inc(v.tensor_scalar(out=cdb64b[:], in0=ones64_b,
                                         scalar1=cdc2[:], scalar2=None, op0=ALU.mult))
            ve_rep = c_ve.n

            pe_cd0 = c_pe.n

            @sec(T)
            def _(t):
                t.wait_ge(c_ve.h, ve_rep)
                for gq in range(8):
                    if gq >= 2:
                        t.wait_ge(c_ve.h, ve_rep + 2 * (gq - 1))
                    bank = 2 + (gq % 2)
                    t.matmul(pf[bank][:], cdb64b[:],
                             rep64_b[:, gq * 512:(gq + 1) * 512],
                             start=True, stop=True, tile_position=TP0)
                    c_pe.inc(t.last)

            @sec(V)
            def _(v):
                for gq in range(8):
                    v.wait_ge(c_pe.h, pe_cd0 + gq + 1)
                    bank = 2 + (gq % 2)
                    v.tensor_tensor(out=S_run[:, gq * 512:(gq + 1) * 512],
                                    in0=S_run[:, gq * 512:(gq + 1) * 512],
                                    in1=pf[bank][:], op=ALU.mult)
                    c_ve.inc(v.last)
                    v.tensor_add(S_run[:, gq * 512:(gq + 1) * 512],
                                 S_run[:, gq * 512:(gq + 1) * 512],
                                 states2[:, gq * 512:(gq + 1) * 512])
                    c_ve.inc(v.last)
            barrier()
        s2s.close()

    scan_es.close()

    if "outproj" in PHASES:
        ops = contextlib.ExitStack()
        psb = lambda n, s, d=F32: ops.enter_context(nc.sbuf_tensor(n, s, d))
        opw = psb("opw", [128, 16 * 32 * 128], BF16)
        yn2 = psb("opyn", [128, 32 * TC], BF16)
        x_f3 = psb("opx", [128, NKT * TC])

        @sec(G)
        def _(g):
            g.wait_ge(c_lg.h, c_lg.n)
            for dte in range(16):
                srcw = w_op[:, dte * 128:(dte + 1) * 128]
                srcw = srcw.rearrange("(kt p) c -> p kt c", p=128)
                c_lg.inc(g.dma_start(out=opw[:, dte * DI:(dte + 1) * DI], in_=srcw), 16)
        lg_op = c_lg.n
        barrier()

        for tc in range(NTC):
            cc0 = tc * TC

            @sec(Y)
            def _(sy, cc0=cc0):
                sy.wait_ge(c_ld.h, c_ld.n)
                c_ld.inc(sy.dma_start(
                    out=yn2[:],
                    in_=yn_dram[:, cc0:cc0 + TC].rearrange("(t p) q -> p t q", p=128)), 16)
                c_ld.inc(sy.dma_start(
                    out=x_f3[:],
                    in_=x_dram[:, cc0:cc0 + TC].rearrange("(kt p) t -> p kt t", p=128)), 16)
            ld_c = c_ld.n

            pe_op0 = c_pe.n
            ve_op0 = c_ve.n

            @sec(T)
            def _(t):
                t.wait_ge(c_ld.h, ld_c)
                t.wait_ge(c_lg.h, lg_op)
                for dte in range(16):
                    if dte >= 2:
                        t.wait_ge(c_ve.h, ve_op0 + dte - 1)
                    bank = dte % 2
                    for kt in range(32):
                        t.matmul(pf[bank][:],
                                 opw[:, dte * DI + kt * 128:dte * DI + (kt + 1) * 128],
                                 yn2[:, kt * TC:(kt + 1) * TC],
                                 start=(kt == 0), stop=(kt == 31))
                    c_pe.inc(t.last)

            @sec(V)
            def _(v):
                for dte in range(16):
                    v.wait_ge(c_pe.h, pe_op0 + dte + 1)
                    bank = dte % 2
                    c_ve.inc(v.tensor_tensor(out=x_f3[:, dte * TC:(dte + 1) * TC],
                                             in0=x_f3[:, dte * TC:(dte + 1) * TC],
                                             in1=pf[bank][:], op=ALU.add))
            ve_opE = c_ve.n

            @sec(Y)
            def _(sy, cc0=cc0):
                sy.wait_ge(c_ve.h, ve_opE)
                sy.wait_ge(c_st.h, c_st.n)
                c_st.inc(sy.dma_start(
                    out=x_dram[:, cc0:cc0 + TC].rearrange("(kt p) t -> p kt t", p=128),
                    in_=x_f3[:]), 16)
                sy.wait_ge(c_st.h, c_st.n)
            barrier()
        ops.close()

    if "ffn2" in PHASES:
        emit_ffn(x_dram, out_ext, "F2")
    else:
        @sec(Y)
        def _(sy):
            sy.wait_ge(c_st.h, c_st.n)
            c_st.inc(sy.dma_start(out=out_ext, in_=x_dram), 16)
        barrier()

    @sec(Y)
    def _(sy):
        sy.wait_ge(c_st.h, c_st.n)
    barrier()
    es.close()
    return nc


def _host_consts(core_id):
    cf = np.zeros((128, 1024), np.float32)
    cf[:, 2] = EPS
    cf[:, 3] = 1.0
    cf[0:1, 374:502] = 1.0
    cf[:, 512:640] = np.eye(128, dtype=np.float32)
    cf[:, 372] = 1.0 if core_id % 2 == 1 else 0.0
    cf2 = np.zeros((128, 640), np.float32)
    s = np.arange(128)[:, None]
    q = np.arange(256)[None, :]
    cf2[:, 0:256] = np.where(q >= s, 0.0, -1e30)
    cf2[:, 256:512] = np.where(q >= s + 128, 0.0, -1e30)
    cf2[0, 512:640] = 1.0
    cf2[32, 512:640] = 1.0
    cb = np.zeros((128, 8192), np.float32)
    for e in range(4):
        cb[e, e * 128:(e + 1) * 128] = 1.0
    cb[:, 640:768] = 1.0 / 2048
    cb[:, 768:896] = 1.0 / 4096
    cb[:, 1024:1152] = np.eye(128)
    cb[:, 1280:1536] = (q >= s)
    cb[:, 1536:1792] = (q >= s + 128)
    cb[0:64, 1792:1920] = 1.0
    cb[0, 896:1024] = 1.0
    cb[32, 896:1024] = 1.0
    h = np.arange(64)[:, None]
    jj = np.arange(4096)[None, :]
    cb[0:64, 2048:6144] = (jj // 64 == h)
    return cf, cf2, cb


def kernel(L0_attn_u, L0_attn_in_proj_w, L0_attn_conv_w, L0_attn_conv_b,
           L0_attn_dt_bias, L0_attn_A_log, L0_attn_D, L0_attn_norm_w,
           L0_attn_out_proj_w, L0_ffn_gate_w, L0_ffn_w1, L0_ffn_b1,
           L0_ffn_w2, L0_ffn_b2):
    u = np.ascontiguousarray(np.asarray(L0_attn_u, np.float32))
    w_ip = np.ascontiguousarray(np.asarray(L0_attn_in_proj_w, np.float32).T)
    w_op = np.ascontiguousarray(np.asarray(L0_attn_out_proj_w, np.float32).T)
    w1 = [np.ascontiguousarray(np.asarray(L0_ffn_w1[e], np.float32)) for e in range(NE)]
    w2 = [np.ascontiguousarray(np.asarray(L0_ffn_w2[e], np.float32)) for e in range(NE)]
    b2 = np.ascontiguousarray(np.asarray(L0_ffn_b2, np.float32))
    gate = np.asarray(L0_ffn_gate_w, np.float32)
    b1 = np.asarray(L0_ffn_b1, np.float32)
    convw = np.asarray(L0_attn_conv_w, np.float32)
    convb = np.asarray(L0_attn_conv_b, np.float32)
    dtb = np.asarray(L0_attn_dt_bias, np.float32)
    alog = np.asarray(L0_attn_A_log, np.float32)
    Dv = np.asarray(L0_attn_D, np.float32)
    nw = np.asarray(L0_attn_norm_w, np.float32)

    in_maps = []
    for core in range(8):
        b, half = divmod(core, 2)
        uT = np.ascontiguousarray(u[b, half * RT:(half + 1) * RT, :].T)
        cf, cf2, cb = _host_consts(core)
        cf[:, 8:144] = convw.reshape(34, 128, 4).transpose(1, 0, 2).reshape(128, 136)
        cf[:, 144:178] = convb.reshape(34, 128).T
        cf[:, 178:242] = b1.reshape(4 * 16, 128).T
        cf[:, 242:306] = gate.T.reshape(16, 128, 4).transpose(1, 0, 2).reshape(128, 64)
        cf[0:64, 306] = dtb
        cf[0:64, 307] = alog
        cf[:, 308:340] = np.repeat(Dv, 64).reshape(32, 128).T
        cf[:, 340:372] = nw.reshape(32, 128).T
        m = {"u_ext": uT, "w_ip": w_ip, "w_op": w_op,
             "cst_f": cf, "cst_f2": cf2, "cst_b": cb.astype(BF), "b2_ext": b2}
        for e in range(NE):
            m[f"w1_{e}"] = w1[e]
            m[f"w2_{e}"] = w2[e]
        in_maps.append(m)

    nc = build_nc()
    res = run_bass_kernel_spmd(nc, in_maps, core_ids=list(range(8)))
    out = np.empty((4, 4096, DM), np.float32)
    for core in range(8):
        b, half = divmod(core, 2)
        out[b, half * RT:(half + 1) * RT, :] = np.asarray(res.results[core]["out_ext"]).T
    return out


# version 11 (final)
# speedup vs baseline: 1.2751x; 1.2751x over previous
import contextlib

import numpy as np
import ml_dtypes

import concourse.bass as bass
import concourse.mybir as mybir
from concourse.bass_utils import run_bass_kernel_spmd

F32 = mybir.dt.float32
BF16 = mybir.dt.bfloat16
AF = mybir.ActivationFunctionType
ALU = mybir.AluOpType
AX = mybir.AxisListType

DM, DI, NH, HD, DS = 2048, 4096, 64, 64, 128
CHK, NCHK, CVD, DIP, RT = 256, 8, 4352, 8512, 2048
NKT, NE, TC, NTC = 16, 4, 512, 4
EPS = 1e-5
BF = ml_dtypes.bfloat16

PHASES = ("ffn1", "halo", "ln2", "inproj", "s1", "s2", "outproj", "ffn2")
DBG_DUMP = None
DBG_FFN = False


_OPS = {
    "matmul", "transpose", "ldweights", "activation", "copy", "add", "mul",
    "tensor_copy", "tensor_tensor", "tensor_add", "tensor_sub", "tensor_mul",
    "tensor_max", "tensor_scalar", "tensor_scalar_add", "tensor_scalar_sub",
    "tensor_scalar_mul", "tensor_scalar_max", "tensor_scalar_min",
    "scalar_tensor_tensor", "tensor_tensor_scan", "reduce_max", "reduce_sum",
    "tensor_reduce", "reciprocal", "select", "memset", "stream_shuffle",
    "dma_start", "collective_compute",
}


class EngW:
    def __init__(self, e):
        self.e = e
        self.last = None

    def __getattr__(self, name):
        a = getattr(self.e, name)
        if name in _OPS:
            def wrap(*args, **kw):
                r = a(*args, **kw)
                self.last = r
                return r
            return wrap
        return a


class Ctr:
    def __init__(self, es, nc, name):
        self.h = es.enter_context(nc.semaphore(name))
        self.n = 0

    def inc(self, instr, k=1):
        instr.then_inc(self.h, k)
        self.n += k
        return self.n


def build_nc():
    nc = bass.Bass("TRN2", target_bir_lowering=False)
    es = contextlib.ExitStack()

    dt_ = nc.dram_tensor
    u_ext = dt_("u_ext", [DM, RT], F32, kind="ExternalInput").ap()
    w_ip = dt_("w_ip", [DM, DIP], F32, kind="ExternalInput").ap()
    w_op = dt_("w_op", [DI, DM], F32, kind="ExternalInput").ap()
    w1_ext = [dt_(f"w1_{e}", [DM, DM], F32, kind="ExternalInput").ap() for e in range(NE)]
    w2_ext = [dt_(f"w2_{e}", [DM, DM], F32, kind="ExternalInput").ap() for e in range(NE)]
    cst_f_ext = dt_("cst_f", [128, 1024], F32, kind="ExternalInput").ap()
    cst_f2_ext = dt_("cst_f2", [128, 640], F32, kind="ExternalInput").ap()
    cst_b_ext = dt_("cst_b", [128, 8192], BF16, kind="ExternalInput").ap()
    b2_ext = dt_("b2_ext", [4, DM], F32, kind="ExternalInput").ap()
    out_ext = dt_("out_ext", [DM, RT], F32, kind="ExternalOutput").ap()
    if DBG_FFN:
        dbg_stat = dt_("dbg_stat", [128, 1536], F32, kind="ExternalOutput").ap()
        dbg_xln = dt_("dbg_xln", [128, 1024], F32, kind="ExternalOutput").ap()
        dbg_wbc = dt_("dbg_wbc", [128, 2048], F32, kind="ExternalOutput").ap()
        dbg_h1w = dt_("dbg_h1w", [128, 1024], F32, kind="ExternalOutput").ap()
        dbg_eL = dt_("dbg_eL", [128, 32], F32, kind="ExternalOutput").ap()
        dbg_xln2 = dt_("dbg_xln2", [128, 1024], F32, kind="ExternalOutput").ap()
        dbg_lg = dt_("dbg_lg", [128, 16], F32, kind="ExternalOutput").ap()
        dbg_nmc = dt_("dbg_nmc", [128, 16], F32, kind="ExternalOutput").ap()

    w1c = dt_("w1c_dram", [128, 64 * 2048], BF16).ap()
    w2c = dt_("w2c_dram", [128, 16 * 8192], BF16).ap()
    x_dram = dt_("x_dram", [DM, RT], F32).ap()
    z_dram = dt_("z_dram", [DI, RT], BF16).ap()
    xcb_dram = dt_("xcb_dram", [CVD, RT], BF16).ap()
    xt_dram = dt_("xt_dram", [RT, DI], BF16).ap()
    st_dram = dt_("st_dram", [NCHK * 128, DI], BF16).ap()
    yn_dram = dt_("yn_dram", [DI, RT], BF16).ap()
    ac_dram = dt_("ac_dram", [NH, CHK], F32).ap()
    ea_dram = dt_("ea_dram", [NH, CHK], F32).ap()
    hcc_in = dt_("hcc_in", [128, 48], F32)
    hcc_out = dt_("hcc_out", [256, 48], F32)
    scc_in = dt_("scc_in", [128, DI], F32)
    scc_out = dt_("scc_out", [256, DI], F32)

    C = lambda n: Ctr(es, nc, n)
    c_ld = C("c_ld")
    c_ldx = C("c_ldx")
    c_stw = C("c_stw")
    c_sw2 = C("c_sw2")
    c_st = C("c_st")
    c_sta = C("c_sta")
    c_stb = C("c_stb")
    c_lg = C("c_lg")
    c_lga = C("c_lga")
    c_lgb = C("c_lgb")
    c_pe = C("c_pe")
    c_se = C("c_se")
    c_ve = C("c_ve")
    c_evA = C("c_evA")
    c_cc1 = C("c_cc1")
    c_cc2 = C("c_cc2")
    c_bar = C("c_bar")
    bar_k = [0]

    sb = lambda n, s, d=F32: es.enter_context(nc.sbuf_tensor(n, s, d))
    cst_f = sb("cst_fS", [128, 1024], F32)
    cst_f2 = sb("cst_f2S", [128, 640], F32)
    cstb0 = sb("cstb0S", [128, 1024], BF16)
    identb_ffn = sb("identb_ffn", [128, 128], BF16)
    b2_b = sb("b2_b", [4, DM], BF16)
    gateT_b = sb("gateT_b", [128, 64], BF16)
    acol = sb("acol", [64, 1], F32)
    xhalo = sb("xhalo", [128, 48], F32)
    xln2h = sb("xln2h", [128, 48], BF16)

    eps_col = cst_f[:, 2:3]
    one_col64 = cst_f[0:64, 3:4]
    convw = lambda t, k: cst_f[:, 8 + t * 4 + k:9 + t * 4 + k]
    convb = lambda t: cst_f[:, 144 + t:145 + t]
    b1col = lambda i: cst_f[:, 178 + i:179 + i]
    gateT_f = cst_f[:, 242:306]
    dtb_col = cst_f[0:64, 306:307]
    alog_col = cst_f[0:64, 307:308]
    drep = lambda t: cst_f[:, 308 + t:309 + t]
    nwcol = lambda t: cst_f[:, 340 + t:341 + t]
    smask = cst_f[:, 372:373]
    onesr_f = cst_f[0:1, 374:502]
    ident_f = cst_f[:, 512:640]
    dmask_f = cst_f2[:, 0:512]
    ones4 = lambda qd: cst_f2[32 * (qd & 1):32 * (qd & 1) + 1, 512:640]

    sel4_b = lambda e: cstb0[0:4, e * 128:(e + 1) * 128]
    onesb4 = lambda qd: cstb0[32 * (qd & 1):32 * (qd & 1) + 1, 896:1024]
    onesS_b = cstb0[:, 640:768]
    onesSi_b = cstb0[:, 768:896]

    pf = [es.enter_context(nc.psum_tensor(f"pf{i}", [128, 512], F32)) for i in range(6)]
    pb = [es.enter_context(nc.psum_tensor(f"pb{i}", [128, 1024], BF16)) for i in range(2)]

    block = es.enter_context(nc.Block())
    V, S, T, Y, G = "vector", "scalar", "tensor", "sync", "gpsimd"

    def sec(e):
        raw = getattr(block, e)

        def deco(fn):
            return raw(lambda eng, _f=fn: _f(EngW(eng)))
        return deco

    _own = {"tensor": None, "vector": None, "scalar": None}

    def barrier():
        bar_k[0] += 1
        tgt = 5 * bar_k[0]
        drains = {T: [(c_pe, c_pe.n)], V: [(c_ve, c_ve.n), (c_evA, c_evA.n)],
                  S: [(c_se, c_se.n)], Y: [], G: []}
        for name in (T, V, S, Y, G):
            @sec(name)
            def _(eng, _t=tgt, _d=drains[name]):
                for ctr, val in _d:
                    eng.wait_ge(ctr.h, val)
                eng.nop().then_inc(c_bar.h, 1)
                eng.wait_ge(c_bar.h, _t)
        c_bar.n = tgt

    TP0 = (0, 0)

    @sec(Y)
    def _(sy):
        c_ld.inc(sy.dma_start(out=cst_f[:], in_=cst_f_ext), 16)
        c_ld.inc(sy.dma_start(out=cst_f2[:], in_=cst_f2_ext), 16)
        c_ld.inc(sy.dma_start(out=cstb0[:], in_=cst_b_ext[:, 0:1024]), 16)
        c_ld.inc(sy.dma_start(out=identb_ffn[:], in_=cst_b_ext[:, 1024:1152]), 16)
    ld0 = c_ld.n

    @sec(G)
    def _(g):
        c_lg.inc(g.dma_start(out=b2_b[:], in_=b2_ext, max_dma_last_dim=4096), 16)
        g.wait_ge(c_lg.h, c_lg.n)

    @sec(S)
    def _(s):
        s.wait_ge(c_ld.h, ld0)
        c_se.inc(s.activation(out=acol[:], in_=alog_col, func=AF.Exp))
    se0 = c_se.n

    @sec(V)
    def _(v):
        v.wait_ge(c_se.h, se0)
        v.tensor_scalar(out=acol[:], in0=acol[:], scalar1=-1.0, scalar2=None,
                        op0=ALU.mult)
        c_ve.inc(v.tensor_copy(out=gateT_b[:], in_=gateT_f))
    barrier()

    def emit_ffn(src, dst, tag):
        ph = contextlib.ExitStack()
        psb = lambda n, s, d=F32: ph.enter_context(nc.sbuf_tensor(n + tag, s, d))
        x_b = psb("x_b", [128, NKT * TC], BF16)
        x_fo = psb("x_fo", [128, 4 * TC])
        xsq = psb("xsq", [128, NKT * TC], BF16)
        xln = psb("xln", [128, NKT * TC], BF16)
        h1s = [psb(f"h1s{i}", [128, TC], BF16) for i in range(2)]
        h1w = psb("h1w", [128, 64 * TC], BF16)
        w1b = [psb(f"w1b{i}", [128, NKT * 128], BF16) for i in range(2)]
        w2b = [psb(f"w2b{i}", [128, 64 * 128], BF16) for i in range(2)]
        stat = psb("stat", [128, 1536])
        MUB = stat[:, 0:512]
        BB = stat[:, 512:1024]
        CC = stat[:, 1024:1536]
        eL = psb("eL", [128, 16])
        nmc = psb("nmc", [128, 4])
        wfT32 = psb("wfT32", [128, 128], BF16)
        t8 = psb("t8", [128, 8])
        t16 = psb("t16", [128, 16])
        g8h = psb("g8h", [128, 8])
        thrq = psb("thrq", [128, 4])
        top1q = psb("top1q", [128, 4])
        giq = psb("giq", [128, 16])
        summq = psb("summq", [128, 4])
        rec4 = psb("rec4", [128, 4])
        wn16 = psb("wn16", [128, 16])
        wrow = psb("wrow", [4, 512], BF16)
        wbc = psb("wbc", [128, NE * TC], BF16)
        outs2 = psb("outs2", [128, 2 * TC])

        w1rel = [0, 0]
        w2rel = [0, 0]
        x_rel = 0
        for tc in range(NTC):
            cc0 = tc * TC

            @sec(G)
            def _(g, cc0=cc0, xr=x_rel):
                g.wait_ge(c_ve.h, xr)
                g.wait_ge(c_lg.h, c_lg.n)
                c_lg.inc(g.dma_start(
                    out=x_b[:],
                    in_=src[:, cc0:cc0 + TC].rearrange("(kt p) t -> p kt t", p=128)), 16)
            ldxb_r = c_lg.n

            @sec(S)
            def _(s, xr=x_rel):
                s.wait_ge(c_lg.h, ldxb_r)
                s.wait_ge(c_ve.h, xr)
                c_se.inc(s.activation(out=xsq[:], in_=x_b[:], func=AF.Square))
            se_sq = c_se.n

            @sec(T)
            def _(t, xr=x_rel):
                t.wait_ge(c_lg.h, ldxb_r)
                t.wait_ge(c_se.h, se_sq)
                t.wait_ge(c_ve.h, xr)
                for kt in range(NKT):
                    t.matmul(pf[0][:], onesS_b, x_b[:, kt * TC:(kt + 1) * TC],
                             start=(kt == 0), stop=(kt == NKT - 1))
                for kt in range(NKT):
                    t.matmul(pf[1][:], onesS_b, xsq[:, kt * TC:(kt + 1) * TC],
                             start=(kt == 0), stop=(kt == NKT - 1))
                c_pe.inc(t.last)
            pe_st = c_pe.n

            @sec(V)
            def _(v):
                v.wait_ge(c_pe.h, pe_st)
                v.tensor_copy(out=MUB, in_=pf[0][:])
                v.scalar_tensor_tensor(out=BB, in0=MUB, scalar=-1.0, in1=MUB,
                                       op0=ALU.mult, op1=ALU.mult)
                c_ve.inc(v.tensor_tensor(out=BB, in0=BB, in1=pf[1][:], op=ALU.add))
            ve_var = c_ve.n

            @sec(S)
            def _(s):
                s.wait_ge(c_ve.h, ve_var)
                s.activation(out=CC, in_=BB, func=AF.Ln, bias=eps_col)
                c_se.inc(s.activation(out=BB, in_=CC, func=AF.Exp, scale=-0.5))
            se_rstd = c_se.n

            @sec(V)
            def _(v):
                v.wait_ge(c_se.h, se_rstd)
                c_ve.inc(v.scalar_tensor_tensor(out=CC, in0=MUB, scalar=-1.0,
                                                in1=BB, op0=ALU.mult, op1=ALU.mult))
            ve_c0 = c_ve.n

            @sec(V)
            def _(v):
                v.wait_ge(c_ve.h, ve_c0)
                for kt in range(NKT):
                    v.tensor_tensor(out=xln[:, kt * TC:(kt + 1) * TC],
                                    in0=x_b[:, kt * TC:(kt + 1) * TC],
                                    in1=BB, op=ALU.mult)
                for kt in range(NKT):
                    v.tensor_add(xln[:, kt * TC:(kt + 1) * TC],
                                 xln[:, kt * TC:(kt + 1) * TC], CC)
                c_ve.inc(v.last)
            ve_xln = c_ve.n

            @sec(T)
            def _(t):
                t.wait_ge(c_ve.h, ve_xln)
                for ti in range(4):
                    for kt in range(NKT):
                        t.matmul(pf[2][:, ti * 4:(ti + 1) * 4],
                                 xln[:, kt * TC + ti * 128:kt * TC + (ti + 1) * 128],
                                 gateT_b[:, kt * 4:(kt + 1) * 4],
                                 start=(ti == 0 and kt == 0),
                                 stop=(ti == 3 and kt == NKT - 1))
                c_pe.inc(t.last)
            pe_g = c_pe.n

            @sec(V)
            def _(v):
                v.wait_ge(c_pe.h, pe_g)
                for i in range(4):
                    v.reduce_max(out=t8[:, i:i + 1], in_=pf[2][:, i * 4:(i + 1) * 4],
                                 axis=AX.X)
                c_ve.inc(v.last)
                v.wait_ge(c_ve.h, c_ve.n)
                c_ve.inc(v.tensor_scalar(out=nmc[:], in0=t8[:, 0:4],
                                         scalar1=-1.0, scalar2=None, op0=ALU.mult))
            ve_nm = c_ve.n

            @sec(S)
            def _(s):
                s.wait_ge(c_ve.h, ve_nm)
                for i in range(4):
                    s.activation(out=eL[:, i * 4:(i + 1) * 4],
                                 in_=pf[2][:, i * 4:(i + 1) * 4],
                                 func=AF.Exp, bias=nmc[:, i:i + 1])
                c_se.inc(s.last)
            se_eL = c_se.n

            @sec(V)
            def _(v):
                v.wait_ge(c_se.h, se_eL)
                e = eL
                for i in range(4):
                    v.tensor_tensor(out=t16[:, i:i + 1], in0=e[:, 4 * i:4 * i + 1],
                                    in1=e[:, 4 * i + 1:4 * i + 2], op=ALU.max)
                for i in range(4):
                    v.tensor_tensor(out=t16[:, 8 + i:9 + i], in0=e[:, 4 * i + 2:4 * i + 3],
                                    in1=e[:, 4 * i + 3:4 * i + 4], op=ALU.max)
                for i in range(4):
                    v.tensor_tensor(out=t16[:, 4 + i:5 + i], in0=e[:, 4 * i:4 * i + 1],
                                    in1=e[:, 4 * i + 1:4 * i + 2], op=ALU.min)
                for i in range(4):
                    v.tensor_tensor(out=t16[:, 12 + i:13 + i], in0=e[:, 4 * i + 2:4 * i + 3],
                                    in1=e[:, 4 * i + 3:4 * i + 4], op=ALU.min)
                c_ve.inc(v.last)
                v.wait_ge(c_ve.h, c_ve.n)
                for i in range(4):
                    v.tensor_tensor(out=g8h[:, i:i + 1], in0=t16[:, i:i + 1],
                                    in1=t16[:, 8 + i:9 + i], op=ALU.min)
                for i in range(4):
                    v.tensor_tensor(out=g8h[:, 4 + i:5 + i], in0=t16[:, 4 + i:5 + i],
                                    in1=t16[:, 12 + i:13 + i], op=ALU.max)
                for i in range(4):
                    v.tensor_tensor(out=top1q[:, i:i + 1], in0=t16[:, i:i + 1],
                                    in1=t16[:, 8 + i:9 + i], op=ALU.max)
                c_ve.inc(v.last)
                v.wait_ge(c_ve.h, c_ve.n)
                for i in range(4):
                    v.tensor_tensor(out=thrq[:, i:i + 1], in0=g8h[:, i:i + 1],
                                    in1=g8h[:, 4 + i:5 + i], op=ALU.max)
                c_ve.inc(v.last)
                v.wait_ge(c_ve.h, c_ve.n)
                for i in range(4):
                    v.tensor_scalar(out=giq[:, 4 * i:4 * i + 4],
                                    in0=eL[:, 4 * i:4 * i + 4],
                                    scalar1=thrq[:, i:i + 1], scalar2=None,
                                    op0=ALU.is_ge)
                v.tensor_tensor(out=summq[:], in0=top1q[:], in1=thrq[:], op=ALU.add)
                c_ve.inc(v.last)
                v.wait_ge(c_ve.h, c_ve.n)
                v.reciprocal(out=rec4[:], in_=summq[:])
                v.tensor_tensor(out=wn16[:], in0=giq[:], in1=eL[:], op=ALU.mult)
                c_ve.inc(v.last)
                v.wait_ge(c_ve.h, c_ve.n)
                for i in range(4):
                    v.tensor_scalar(out=wfT32[:, i * 32:i * 32 + 4],
                                    in0=wn16[:, 4 * i:4 * i + 4],
                                    scalar1=rec4[:, i:i + 1], scalar2=0.5,
                                    op0=ALU.mult, op1=ALU.mult)
                c_ve.inc(v.last)
            ve_wf = c_ve.n

            @sec(T)
            def _(t):
                t.wait_ge(c_ve.h, ve_wf)
                for i in range(4):
                    t.transpose(pb[0][0:32, i * 128:(i + 1) * 128],
                                wfT32[:, i * 32:(i + 1) * 32], identb_ffn[:])
                c_pe.inc(t.last)
            pe_wr = c_pe.n

            @sec(V)
            def _(v):
                v.wait_ge(c_pe.h, pe_wr)
                c_ve.inc(v.tensor_copy(out=wrow[:], in_=pb[0][0:4, 0:512]))
            ve_wrow = c_ve.n

            pe_wbc0 = c_pe.n
            ve_wbc0 = c_ve.n

            @sec(T)
            def _(t):
                t.wait_ge(c_ve.h, ve_wrow)
                for e in range(NE):
                    if e > 0:
                        t.wait_ge(c_ve.h, ve_wbc0 + e)
                    t.matmul(pf[5][:], sel4_b(e), wrow[:],
                             start=True, stop=True, tile_position=TP0)
                    c_pe.inc(t.last)

            @sec(V)
            def _(v):
                for e in range(NE):
                    v.wait_ge(c_pe.h, pe_wbc0 + e + 1)
                    c_ve.inc(v.tensor_copy(out=wbc[:, e * TC:(e + 1) * TC],
                                           in_=pf[5][:]))
            ve_wbc = c_ve.n

            pe_m1 = c_pe.n
            se_m1 = c_se.n
            ve_m1 = c_ve.n

            lw1tot = []
            cache_wr = (tag == "F1" and tc == 0)
            cache_rd = (tag == "F2") or (tag == "F1" and tc >= 1)
            stw_m1_0 = c_stw.n
            stw_all = c_stw.n
            sw2_all = c_sw2.n

            @sec(G)
            def _(g):
                for i in range(NE * NKT):
                    e, ht = divmod(i, NKT)
                    par = c_lga if i % 2 == 0 else c_lgb
                    if i >= 2:
                        g.wait_ge(c_pe.h, pe_m1 + i - 1)
                        if cache_wr:
                            g.wait_ge(c_stw.h, stw_m1_0 + 16 * (i - 1))
                    else:
                        g.wait_ge(c_pe.h, w1rel[i % 2])
                        if cache_rd and tag == "F1" and tc == 1:
                            g.wait_ge(c_stw.h, stw_all)
                            g.wait_ge(c_sw2.h, sw2_all)
                    g.wait_ge(par.h, par.n)
                    if cache_rd:
                        srcw = w1c[:, i * 2048:(i + 1) * 2048]
                    else:
                        srcw = w1_ext[e][:, ht * 128:(ht + 1) * 128]
                        srcw = srcw.rearrange("(kt p) c -> p kt c", p=128)
                    par.inc(g.dma_start(out=w1b[i % 2][:], in_=srcw), 16)
                    lw1tot.append((par, par.n))

            if cache_wr:
                @sec(Y)
                def _(sy):
                    for i in range(NE * NKT):
                        sy.wait_ge(lw1tot[i][0].h, lw1tot[i][1])
                        c_stw.inc(sy.dma_start(out=w1c[:, i * 2048:(i + 1) * 2048],
                                               in_=w1b[i % 2][:]), 16)

            @sec(T)
            def _(t):
                t.wait_ge(c_ve.h, ve_wbc)
                for i in range(NE * NKT):
                    t.wait_ge(lw1tot[i][0].h, lw1tot[i][1])
                    if i >= 2:
                        t.wait_ge(c_se.h, se_m1 + i - 1)
                    for kt in range(NKT):
                        t.matmul(pf[i % 2][:],
                                 w1b[i % 2][:, kt * 128:(kt + 1) * 128],
                                 xln[:, kt * TC:(kt + 1) * TC],
                                 start=(kt == 0), stop=(kt == NKT - 1))
                    c_pe.inc(t.last)
                    w1rel[i % 2] = c_pe.n

            @sec(S)
            def _(s):
                for i in range(NE * NKT):
                    s.wait_ge(c_pe.h, pe_m1 + i + 1)
                    if i >= 2:
                        s.wait_ge(c_ve.h, ve_m1 + i - 1)
                    c_se.inc(s.activation(out=h1s[i % 2][:], in_=pf[i % 2][:],
                                          func=AF.Gelu, bias=b1col(i)))

            @sec(V)
            def _(v):
                for i in range(NE * NKT):
                    e = i // NKT
                    v.wait_ge(c_se.h, se_m1 + i + 1)
                    c_ve.inc(v.tensor_tensor(out=h1w[:, i * TC:(i + 1) * TC],
                                             in0=h1s[i % 2][:],
                                             in1=wbc[:, e * TC:(e + 1) * TC],
                                             op=ALU.mult))
            ve_m1e = c_ve.n

            if DBG_FFN and tag == "F1" and tc == 0:
                dlg = ph.enter_context(nc.sbuf_tensor("dlg" + tag, [128, 16], F32))

                @sec(V)
                def _(v):
                    v.wait_ge(c_pe.h, pe_g)
                    c_ve.inc(v.tensor_copy(out=dlg[:], in_=pf[2][:, 0:16]))
                dlg_ve = c_ve.n

                @sec(G)
                def _(g):
                    g.wait_ge(c_ve.h, ve_xln)
                    c_lg.inc(g.dma_start(out=dbg_xln2, in_=xln[:, 0:1024]), 16)
                    g.wait_ge(c_lg.h, c_lg.n)
                    g.wait_ge(c_ve.h, ve_m1e)
                    c_lg.inc(g.dma_start(out=dbg_xln, in_=xln[:, 0:1024]), 16)
                    c_lg.inc(g.dma_start(out=dbg_wbc, in_=wbc[:, 0:2048]), 16)
                    c_lg.inc(g.dma_start(out=dbg_h1w, in_=h1w[:, 0:1024]), 16)
                    c_lg.inc(g.dma_start(out=dbg_eL[:, 0:16], in_=eL[:]), 16)
                    c_lg.inc(g.dma_start(out=dbg_eL[:, 16:32], in_=wfT32[:, 0:16]), 16)
                    g.wait_ge(c_ve.h, dlg_ve)
                    c_lg.inc(g.dma_start(out=dbg_lg, in_=dlg[:]), 16)
                    c_lg.inc(g.dma_start(out=dbg_nmc[:, 0:4], in_=nmc[:]), 16)
                    c_lg.inc(g.dma_start(out=dbg_nmc[:, 4:12], in_=t8[:]), 16)
                    g.wait_ge(c_lg.h, c_lg.n)

                @sec(Y)
                def _(sy):
                    sy.wait_ge(c_ve.h, ve_m1e)
                    c_st.inc(sy.dma_start(out=dbg_stat, in_=stat[:]), 16)
                    sy.wait_ge(c_st.h, c_st.n)

            pe_m2 = c_pe.n
            ve_m2 = c_ve.n
            xfo_tot = []
            lw2tot = []
            sw2_0 = c_sw2.n

            @sec(G)
            def _(g, cc0=cc0):
                for ot in range(NKT):
                    if ot % 4 == 0:
                        bq = ot // 4
                        if bq >= 1:
                            g.wait_ge(c_ve.h, ve_m2 + 4 * bq)
                        g.wait_ge(c_ldx.h, c_ldx.n)
                        for k in range(4):
                            oo = bq * 4 + k
                            c_ldx.inc(g.dma_start(
                                out=x_fo[:, (oo % 4) * TC:(oo % 4 + 1) * TC],
                                in_=src[oo * 128:(oo + 1) * 128, cc0:cc0 + TC]), 16)
                        xfo_tot.append(c_ldx.n)
                    par = c_lga if ot % 2 == 0 else c_lgb
                    if ot >= 2:
                        g.wait_ge(c_pe.h, pe_m2 + ot - 1)
                        if cache_wr:
                            g.wait_ge(c_sw2.h, sw2_0 + 16 * (ot - 1))
                    else:
                        g.wait_ge(c_pe.h, w2rel[ot % 2])
                    g.wait_ge(par.h, par.n)
                    if cache_rd:
                        par.inc(g.dma_start(
                            out=w2b[ot % 2][:],
                            in_=w2c[:, ot * 8192:(ot + 1) * 8192]), 16)
                    else:
                        for e in range(NE):
                            srcw = w2_ext[e][:, ot * 128:(ot + 1) * 128]
                            srcw = srcw.rearrange("(kt p) c -> p kt c", p=128)
                            par.inc(g.dma_start(
                                out=w2b[ot % 2][:, e * NKT * 128:(e + 1) * NKT * 128],
                                in_=srcw), 16)
                    lw2tot.append((par, par.n))

            if cache_wr:
                @sec(S)
                def _(s):
                    for ot in range(NKT):
                        s.wait_ge(lw2tot[ot][0].h, lw2tot[ot][1])
                        c_sw2.inc(s.dma_start(
                            out=w2c[:, ot * 8192:(ot + 1) * 8192],
                            in_=w2b[ot % 2][:]), 16)

            @sec(T)
            def _(t):
                t.wait_ge(c_ve.h, ve_m1e)
                for ot in range(NKT):
                    t.wait_ge(lw2tot[ot][0].h, lw2tot[ot][1])
                    if ot >= 2:
                        t.wait_ge(c_ve.h, ve_m2 + ot - 1)
                    bank = 2 + (ot % 2)
                    first = True
                    for e in range(NE):
                        for ht in range(NKT):
                            i = e * NKT + ht
                            t.matmul(pf[bank][:],
                                     w2b[ot % 2][:, i * 128:(i + 1) * 128],
                                     h1w[:, i * TC:(i + 1) * TC],
                                     start=first, stop=False)
                            first = False
                    t.matmul(pf[bank][:],
                             b2_b[0:4, ot * 128:(ot + 1) * 128],
                             wrow[:],
                             start=False, stop=True, tile_position=TP0)
                    c_pe.inc(t.last)
                    w2rel[ot % 2] = c_pe.n

            sta0 = c_sta.n
            stb0 = c_stb.n

            @sec(V)
            def _(v):
                for ot in range(NKT):
                    v.wait_ge(c_pe.h, pe_m2 + ot + 1)
                    v.wait_ge(c_ldx.h, xfo_tot[ot // 4])
                    if ot >= 2:
                        spar, sbase = (c_sta, sta0) if ot % 2 == 0 else (c_stb, stb0)
                        v.wait_ge(spar.h, sbase + 16 * (ot // 2))
                    bank = 2 + (ot % 2)
                    c_ve.inc(v.tensor_tensor(
                        out=outs2[:, (ot % 2) * TC:(ot % 2 + 1) * TC],
                        in0=pf[bank][:], in1=x_fo[:, (ot % 4) * TC:(ot % 4 + 1) * TC],
                        op=ALU.add))
            ve_m2e = c_ve.n

            @sec(Y)
            def _(sy, cc0=cc0):
                for ot in range(NKT):
                    spar = c_sta if ot % 2 == 0 else c_stb
                    sy.wait_ge(c_ve.h, ve_m2 + ot + 1)
                    sy.wait_ge(spar.h, spar.n)
                    spar.inc(sy.dma_start(
                        out=dst[ot * 128:(ot + 1) * 128, cc0:cc0 + TC],
                        in_=outs2[:, (ot % 2) * TC:(ot % 2 + 1) * TC]), 16)
            x_rel = ve_m2e

        @sec(Y)
        def _(sy):
            sy.wait_ge(c_sta.h, c_sta.n)
            sy.wait_ge(c_stb.h, c_stb.n)
            sy.wait_ge(c_ldx.h, c_ldx.n)
            sy.wait_ge(c_stw.h, c_stw.n)
            sy.wait_ge(c_sw2.h, c_sw2.n)
        barrier()
        ph.close()

    if "ffn1" in PHASES:
        emit_ffn(u_ext, x_dram, "F1")
    else:
        @sec(Y)
        def _(sy):
            c_st.inc(sy.dma_start(out=x_dram, in_=u_ext), 16)
            sy.wait_ge(c_st.h, c_st.n)
        barrier()

    scan_es = contextlib.ExitStack()
    dtv_all = scan_es.enter_context(nc.sbuf_tensor("dtv_all", [64, RT], F32))
    S_run = scan_es.enter_context(nc.sbuf_tensor("S_run", [128, DI], F32))
    cstbS = scan_es.enter_context(nc.sbuf_tensor("cstbS", [128, 8192], BF16))
    ident_b = cstbS[:, 1024:1152]
    cbmask_b = cstbS[:, 1280:1792]
    ones64_b = cstbS[0:64, 1792:1920]
    rep64_b = cstbS[0:64, 2048:6144]

    @sec(Y)
    def _(sy):
        c_ld.inc(sy.dma_start(out=cstbS[:], in_=cst_b_ext), 16)
        sy.wait_ge(c_ld.h, c_ld.n)
    barrier()

    if "halo" in PHASES:
        hstage = scan_es.enter_context(nc.sbuf_tensor("hstage", [128, 48], F32))
        hstb = scan_es.enter_context(nc.sbuf_tensor("hstb", [128, 96], BF16))
        hbc = scan_es.enter_context(nc.sbuf_tensor("hbc", [128, 16], F32))

        @sec(G)
        def _(g):
            g.wait_ge(c_lg.h, c_lg.n)
            c_lg.inc(g.dma_start(
                out=hstage[:],
                in_=x_dram[:, RT - 3:RT].rearrange("(kt p) t -> p kt t", p=128)), 16)
            g.wait_ge(c_lg.h, c_lg.n)
            c_lg.inc(g.dma_start(out=hcc_in.ap(), in_=hstage[:]), 16)
            g.wait_ge(c_lg.h, c_lg.n)
            c_cc1.inc(g.collective_compute(
                "AllGather", ALU.bypass,
                replica_groups=[[0, 1], [2, 3], [4, 5], [6, 7]],
                ins=[hcc_in.ap().opt()], outs=[hcc_out.ap().opt()]), 1)
            g.wait_ge(c_cc1.h, c_cc1.n)
            c_lg.inc(g.dma_start(out=xhalo[:], in_=hcc_out.ap()[0:128, :]), 16)
        lg_h = c_lg.n

        @sec(V)
        def _(v):
            v.wait_ge(c_lg.h, lg_h)
            c_ve.inc(v.tensor_scalar(out=hstb[:, 0:48], in0=xhalo[:], scalar1=smask,
                                     scalar2=None, op0=ALU.mult))
        ve_h = c_ve.n

        @sec(S)
        def _(s):
            s.wait_ge(c_ve.h, ve_h)
            c_se.inc(s.activation(out=hstb[:, 48:96], in_=hstb[:, 0:48],
                                  func=AF.Square))
        se_h = c_se.n

        @sec(T)
        def _(t):
            t.wait_ge(c_se.h, se_h)
            for kt in range(NKT):
                t.matmul(pf[0][:, 0:3], onesS_b, hstb[:, kt * 3:(kt + 1) * 3],
                         start=(kt == 0), stop=(kt == NKT - 1))
            for kt in range(NKT):
                t.matmul(pf[1][:, 0:3], onesS_b, hstb[:, 48 + kt * 3:48 + (kt + 1) * 3],
                         start=(kt == 0), stop=(kt == NKT - 1))
            c_pe.inc(t.last)
        pe_h = c_pe.n

        @sec(V)
        def _(v):
            v.wait_ge(c_pe.h, pe_h)
            c_ve.inc(v.tensor_copy(out=hbc[:, 0:3], in_=pf[0][:, 0:3]))
            v.wait_ge(c_ve.h, c_ve.n)
            c_ve.inc(v.scalar_tensor_tensor(out=hbc[:, 4:7], in0=hbc[:, 0:3],
                                            scalar=-1.0, in1=hbc[:, 0:3],
                                            op0=ALU.mult, op1=ALU.mult))
            v.wait_ge(c_ve.h, c_ve.n)
            c_ve.inc(v.tensor_tensor(out=hbc[:, 4:7], in0=hbc[:, 4:7],
                                     in1=pf[1][:, 0:3], op=ALU.add))
        ve_hv = c_ve.n

        @sec(S)
        def _(s):
            s.wait_ge(c_ve.h, ve_hv)
            c_se.inc(s.activation(out=hbc[:, 8:11], in_=hbc[:, 4:7], func=AF.Ln,
                                  bias=eps_col))
            s.wait_ge(c_se.h, c_se.n)
            c_se.inc(s.activation(out=hbc[:, 4:7], in_=hbc[:, 8:11],
                                  func=AF.Exp, scale=-0.5))
        se_hr = c_se.n

        @sec(V)
        def _(v):
            v.wait_ge(c_se.h, se_hr)
            for kt in range(NKT):
                v.tensor_tensor(out=xln2h[:, kt * 3:(kt + 1) * 3],
                                in0=hstb[:, kt * 3:(kt + 1) * 3],
                                in1=hbc[:, 0:3], op=ALU.subtract)
            c_ve.inc(v.last)
            v.wait_ge(c_ve.h, c_ve.n)
            for kt in range(NKT):
                v.tensor_tensor(out=xln2h[:, kt * 3:(kt + 1) * 3],
                                in0=xln2h[:, kt * 3:(kt + 1) * 3],
                                in1=hbc[:, 4:7], op=ALU.mult)
            c_ve.inc(v.last)
        barrier()
    else:
        @sec(V)
        def _(v):
            v.memset(xln2h[:], 0)
            c_ve.inc(v.last)
        barrier()

    mam = contextlib.ExitStack()
    xln2 = mam.enter_context(nc.sbuf_tensor("xln2_all", [128, NKT * RT], BF16))
    if "ln2" in PHASES:
        ln2s = contextlib.ExitStack()
        psb = lambda n, s, d=F32: ln2s.enter_context(nc.sbuf_tensor(n, s, d))
        x_b2 = psb("ln2xb", [128, NKT * TC], BF16)
        xsq2 = psb("ln2sq", [128, NKT * TC], BF16)
        stat2 = psb("ln2st", [128, 1536])
        MUB = stat2[:, 0:512]
        BB = stat2[:, 512:1024]
        CC = stat2[:, 1024:1536]
        ln2_rel = 0
        for tc in range(NTC):
            cc0 = tc * TC

            @sec(G)
            def _(g, cc0=cc0, lr=ln2_rel):
                g.wait_ge(c_ve.h, lr)
                g.wait_ge(c_lg.h, c_lg.n)
                c_lg.inc(g.dma_start(
                    out=x_b2[:],
                    in_=x_dram[:, cc0:cc0 + TC].rearrange("(kt p) t -> p kt t", p=128)), 16)
            ldxb_r = c_lg.n

            @sec(S)
            def _(s, lr=ln2_rel):
                s.wait_ge(c_lg.h, ldxb_r)
                s.wait_ge(c_ve.h, lr)
                c_se.inc(s.activation(out=xsq2[:], in_=x_b2[:], func=AF.Square))
            se_sq = c_se.n

            @sec(T)
            def _(t, lr=ln2_rel):
                t.wait_ge(c_lg.h, ldxb_r)
                t.wait_ge(c_se.h, se_sq)
                t.wait_ge(c_ve.h, lr)
                for kt in range(NKT):
                    t.matmul(pf[0][:], onesS_b, x_b2[:, kt * TC:(kt + 1) * TC],
                             start=(kt == 0), stop=(kt == NKT - 1))
                for kt in range(NKT):
                    t.matmul(pf[1][:], onesS_b, xsq2[:, kt * TC:(kt + 1) * TC],
                             start=(kt == 0), stop=(kt == NKT - 1))
                c_pe.inc(t.last)
            pe_st = c_pe.n

            @sec(V)
            def _(v):
                v.wait_ge(c_pe.h, pe_st)
                v.tensor_copy(out=MUB, in_=pf[0][:])
                v.scalar_tensor_tensor(out=BB, in0=MUB, scalar=-1.0, in1=MUB,
                                       op0=ALU.mult, op1=ALU.mult)
                c_ve.inc(v.tensor_tensor(out=BB, in0=BB, in1=pf[1][:], op=ALU.add))
            ve_var = c_ve.n

            @sec(S)
            def _(s):
                s.wait_ge(c_ve.h, ve_var)
                s.activation(out=CC, in_=BB, func=AF.Ln, bias=eps_col)
                c_se.inc(s.activation(out=BB, in_=CC, func=AF.Exp, scale=-0.5))
            se_rstd = c_se.n

            @sec(V)
            def _(v):
                v.wait_ge(c_se.h, se_rstd)
                c_ve.inc(v.scalar_tensor_tensor(out=CC, in0=MUB, scalar=-1.0,
                                                in1=BB, op0=ALU.mult, op1=ALU.mult))
            ve_c0 = c_ve.n

            @sec(V)
            def _(v, cc0=cc0):
                v.wait_ge(c_ve.h, ve_c0)
                for kt in range(NKT):
                    o = kt * RT + cc0
                    v.tensor_tensor(out=xln2[:, o:o + TC],
                                    in0=x_b2[:, kt * TC:(kt + 1) * TC],
                                    in1=BB, op=ALU.mult)
                for kt in range(NKT):
                    o = kt * RT + cc0
                    v.tensor_add(xln2[:, o:o + TC], xln2[:, o:o + TC], CC)
                c_ve.inc(v.last)
            ln2_rel = c_ve.n
        barrier()
        ln2s.close()

    if "inproj" in PHASES:
        ips = contextlib.ExitStack()
        psb = lambda n, s, d=F32: ips.enter_context(nc.sbuf_tensor(n, s, d))
        wipb = [psb(f"wipb{i}", [128, NKT * 128], BF16) for i in range(2)]
        strip = psb("strip", [128, 515], BF16)
        accb = [psb(f"ipacc{i}", [128, TC], BF16) for i in range(2)]
        stg = [psb(f"ipst{i}", [128, TC], BF16) for i in range(2)]
        dts = psb("dts", [64, 4 * TC])
        NOC = 67
        pe0 = c_pe.n
        evA0 = c_evA.n
        se0_ = c_se.n
        st0a = c_sta.n
        st0b = c_stb.n
        ve_start = c_ve.n

        def is_xbc(oc):
            return 32 <= oc < 66

        def pe_at(oc, tc):
            v = pe0 + sum(NTC + (1 if is_xbc(o) else 0) for o in range(oc))
            if is_xbc(oc):
                v += 1
            return v + tc + 1

        def pe_halo_at(oc):
            return pe0 + sum(NTC + (1 if is_xbc(o) else 0) for o in range(oc)) + 1

        def evA_blk(oc):
            v = evA0
            for o in range(oc):
                v += (1 + 2 * NTC) if is_xbc(o) else NTC
            return v

        def evA_evac(oc, tc):
            if is_xbc(oc):
                return evA_blk(oc) + 1 + 2 * tc + 1
            return evA_blk(oc) + tc + 1

        def evA_conv(oc, tc):
            return evA_blk(oc) + 1 + 2 * tc + 2

        def evA_halo(oc):
            return evA_blk(oc) + 1

        def se_silu(oc, tc):
            n = sum(NTC for o in range(32, oc) if is_xbc(o))
            return se0_ + n + tc + 1

        def st_par(j):
            return (c_sta, st0a) if j % 2 == 0 else (c_stb, st0b)

        def st_done(j):
            sem, base = st_par(j)
            return sem, base + 16 * (j // 2 + 1)

        def ip_src_col(oc):
            return 8384 if oc == NOC - 1 else oc * 128

        lgiptot = []

        @sec(G)
        def _(g):
            for oc in range(NOC):
                par = c_lga if oc % 2 == 0 else c_lgb
                if oc >= 2:
                    g.wait_ge(c_pe.h, pe_at(oc - 2, NTC - 1))
                g.wait_ge(par.h, par.n)
                srcw = w_ip[:, ip_src_col(oc):ip_src_col(oc) + 128]
                srcw = srcw.rearrange("(kt p) c -> p kt c", p=128)
                par.inc(g.dma_start(out=wipb[oc % 2][:], in_=srcw), 16)
                lgiptot.append((par, par.n))

        @sec(T)
        def _(t):
            t.wait_ge(c_ve.h, ve_start)
            for oc in range(NOC):
                t.wait_ge(lgiptot[oc][0].h, lgiptot[oc][1])
                if is_xbc(oc):
                    if oc >= 33:
                        t.wait_ge(c_evA.h, evA_halo(oc - 1))
                    for kt in range(NKT):
                        t.matmul(pf[4][:, 0:3], wipb[oc % 2][:, kt * 128:(kt + 1) * 128],
                                 xln2h[:, kt * 3:(kt + 1) * 3],
                                 start=(kt == 0), stop=(kt == NKT - 1))
                    c_pe.inc(t.last)
                for tc in range(NTC):
                    j = oc * NTC + tc
                    if j >= 2:
                        jo, jt = divmod(j - 2, NTC)
                        t.wait_ge(c_evA.h, evA_evac(jo, jt))
                    for kt in range(NKT):
                        t.matmul(pf[tc % 2][:],
                                 wipb[oc % 2][:, kt * 128:(kt + 1) * 128],
                                 xln2[:, kt * RT + tc * TC:
                                      kt * RT + tc * TC + TC],
                                 start=(kt == 0), stop=(kt == NKT - 1))
                    c_pe.inc(t.last)
        assert c_pe.n == pe_at(NOC - 1, NTC - 1), (c_pe.n, pe_at(NOC - 1, NTC - 1))

        @sec(V)
        def _(v):
            for oc in range(NOC):
                if oc < 32:
                    for tc in range(NTC):
                        v.wait_ge(c_pe.h, pe_at(oc, tc))
                        j = oc * NTC + tc
                        if j >= 2:
                            sem, val = st_done(j - 2)
                            v.wait_ge(sem.h, val)
                        v.tensor_copy(out=stg[tc % 2][:], in_=pf[tc % 2][:])
                        c_evA.inc(v.last)
                elif is_xbc(oc):
                    ti = oc - 32
                    v.wait_ge(c_pe.h, pe_halo_at(oc))
                    v.tensor_copy(out=strip[:, 0:3], in_=pf[4][:, 0:3])
                    c_evA.inc(v.last)
                    v.wait_ge(c_evA.h, c_evA.n)
                    for tc in range(NTC):
                        v.wait_ge(c_pe.h, pe_at(oc, tc))
                        v.tensor_copy(out=strip[:, 3:515], in_=pf[tc % 2][:])
                        c_evA.inc(v.last)
                        v.wait_ge(c_evA.h, c_evA.n)
                        j = oc * NTC + tc
                        if j >= 2:
                            sem, val = st_done(j - 2)
                            v.wait_ge(sem.h, val)
                        v.tensor_scalar(out=accb[tc % 2][:], in0=strip[:, 0:512],
                                        scalar1=convw(ti, 0), scalar2=None,
                                        op0=ALU.mult)
                        for k in range(1, 4):
                            v.scalar_tensor_tensor(out=accb[tc % 2][:],
                                                   in0=strip[:, k:512 + k],
                                                   scalar=convw(ti, k),
                                                   in1=accb[tc % 2][:],
                                                   op0=ALU.mult, op1=ALU.add)
                        v.tensor_copy(out=strip[:, 0:3], in_=strip[:, 512:515])
                        c_evA.inc(v.last)
                        v.wait_ge(c_evA.h, c_evA.n)
                else:
                    for tc in range(NTC):
                        v.wait_ge(c_pe.h, pe_at(oc, tc))
                        v.tensor_scalar(out=dts[:, 0:TC], in0=pf[tc % 2][64:128, :],
                                        scalar1=dtb_col, scalar2=None, op0=ALU.add)
                        c_evA.inc(v.last)
                        v.wait_ge(c_se.h, se_silu(66, 0) + tc)
                        c_ve.inc(v.tensor_add(dtv_all[:, tc * TC:(tc + 1) * TC],
                                              dts[:, TC:2 * TC], dts[:, 2 * TC:3 * TC]))
        assert c_evA.n == evA_blk(66) + NTC, (c_evA.n, evA_blk(66) + NTC)

        @sec(S)
        def _(s):
            for oc in range(32, 66):
                ti = oc - 32
                for tc in range(NTC):
                    s.wait_ge(c_evA.h, evA_conv(oc, tc))
                    j = oc * NTC + tc
                    if j >= 2:
                        sem, val = st_done(j - 2)
                        s.wait_ge(sem.h, val)
                    s.activation(out=stg[tc % 2][:], in_=accb[tc % 2][:],
                                 func=AF.Silu, bias=convb(ti))
                    c_se.inc(s.last)
            for tc in range(NTC):
                s.wait_ge(c_evA.h, evA_evac(66, tc))
                s.activation(out=dts[:, TC:2 * TC], in_=dts[:, 0:TC], func=AF.Relu)
                s.activation(out=dts[:, 2 * TC:3 * TC], in_=dts[:, 0:TC], func=AF.Abs)
                s.activation(out=dts[:, 3 * TC:4 * TC], in_=dts[:, 2 * TC:3 * TC],
                             func=AF.Exp, scale=-1.0)
                s.activation(out=dts[:, 2 * TC:3 * TC], in_=dts[:, 3 * TC:4 * TC],
                             func=AF.Ln, bias=one_col64)
                c_se.inc(s.last)
        assert c_se.n == se0_ + 136 + NTC, (c_se.n, se0_ + 136 + NTC)

        @sec(Y)
        def _(sy):
            for oc in range(66):
                for tc in range(NTC):
                    j = oc * NTC + tc
                    sem, base = st_par(j)
                    if oc < 32:
                        sy.wait_ge(c_evA.h, evA_evac(oc, tc))
                        dstp = z_dram[oc * 128:(oc + 1) * 128, tc * TC:(tc + 1) * TC]
                    else:
                        sy.wait_ge(c_se.h, se_silu(oc, tc))
                        ti = oc - 32
                        dstp = xcb_dram[ti * 128:(ti + 1) * 128, tc * TC:(tc + 1) * TC]
                    sy.wait_ge(sem.h, sem.n)
                    sem.inc(sy.dma_start(out=dstp, in_=stg[tc % 2][:]), 16)
        assert c_sta.n == st0a + 16 * 132 and c_stb.n == st0b + 16 * 132

        @sec(Y)
        def _(sy):
            sy.wait_ge(c_sta.h, c_sta.n)
            sy.wait_ge(c_stb.h, c_stb.n)
        barrier()
        ips.close()
    mam.close()

    if "s1" in PHASES:
        s1s = contextlib.ExitStack()
        psb = lambda n, s, d=F32: s1s.enter_context(nc.sbuf_tensor(n, s, d))
        xch = psb("s1xch", [128, 32 * CHK], BF16)
        Bt_in = psb("s1B", [128, CHK], BF16)
        xT_sb = psb("s1xT", [128, 2 * DI], BF16)
        xds = psb("s1xds", [128, 2 * DI], BF16)
        BT_sb = psb("s1BT", [128, 256], BF16)
        ones_t = psb("s1ones", [64, CHK])
        ac_t = psb("s1ac", [64, CHK])
        acum_t = psb("s1acum", [128, CHK])
        dec_t = psb("s1dec", [64, CHK])
        dd_t = psb("s1dd", [128, CHK])
        ddT = psb("s1ddT", [128, 128])
        cdc = psb("s1cdc", [64, 1])
        cdb64a = psb("s1cdb", [64, 128], BF16)
        states_sb = psb("s1st", [128, DI])

        @sec(V)
        def _(v):
            v.memset(ones_t[:], 1.0)
            v.memset(S_run[:], 0.0)
            v.memset(dd_t[:], 0.0)
            v.memset(acum_t[:], 0.0)
            c_ve.inc(v.last)
        barrier()

        for c in range(NCHK):
            q0 = c * CHK

            @sec(Y)
            def _(sy, q0=q0):
                sy.wait_ge(c_ld.h, c_ld.n)
                c_ld.inc(sy.dma_start(
                    out=xch[:],
                    in_=xcb_dram[0:DI, q0:q0 + CHK].rearrange("(t p) q -> p t q", p=128)), 16)
                c_ld.inc(sy.dma_start(out=Bt_in[:],
                                      in_=xcb_dram[DI:DI + 128, q0:q0 + CHK]), 16)
            ld_c = c_ld.n

            @sec(V)
            def _(v, c=c):
                c_ve.inc(v.tensor_scalar(out=ac_t[:], in0=dtv_all[:, c * CHK:(c + 1) * CHK],
                                         scalar1=acol[:], scalar2=None, op0=ALU.mult))
                v.wait_ge(c_ve.h, c_ve.n)
                c_ve.inc(v.tensor_tensor_scan(out=acum_t[0:64, :], data0=ones_t[:],
                                              data1=ac_t[:], initial=0.0,
                                              op0=ALU.mult, op1=ALU.add))
            ve_ac = c_ve.n

            @sec(S)
            def _(s):
                s.wait_ge(c_ve.h, ve_ac)
                s.activation(out=cdc[:], in_=acum_t[0:64, CHK - 1:CHK], func=AF.Exp)
                c_se.inc(s.activation(out=dec_t[:], in_=acum_t[0:64, :], func=AF.Exp,
                                      scale=-1.0, bias=acum_t[0:64, CHK - 1:CHK]))
            se_dec = c_se.n

            @sec(V)
            def _(v, c=c):
                v.wait_ge(c_se.h, se_dec)
                c_ve.inc(v.tensor_tensor(out=dd_t[0:64, :], in0=dec_t[:],
                                         in1=dtv_all[:, c * CHK:(c + 1) * CHK],
                                         op=ALU.mult))
            ve_dd = c_ve.n

            @sec(T)
            def _(t):
                t.wait_ge(c_ve.h, ve_dd)
                for sg in range(2):
                    t.transpose(pf[0][:, sg * 128:(sg + 1) * 128],
                                dd_t[:, sg * 128:(sg + 1) * 128], ident_f)
                c_pe.inc(t.last)
            pe_ddT = c_pe.n

            @sec(V)
            def _(v):
                v.wait_ge(c_pe.h, pe_ddT)
                v.tensor_copy(out=ddT[:, 0:64], in_=pf[0][:, 0:64])
                c_ve.inc(v.tensor_copy(out=ddT[:, 64:128], in_=pf[0][:, 128:192]))
            ve_ddT = c_ve.n

            pe_xt0 = c_pe.n
            ve_xt0 = c_ve.n

            @sec(T)
            def _(t):
                t.wait_ge(c_ld.h, ld_c)
                gidx = 0
                for sg in range(2):
                    for pt in range(32):
                        bkt = gidx % 2
                        if gidx >= 2 and pt % 8 == 0:
                            t.wait_ge(c_ve.h, ve_xt0 + gidx - 1)
                        t.transpose(pb[bkt][:, (pt % 8) * 128:(pt % 8 + 1) * 128],
                                    xch[:, pt * CHK + sg * 128:pt * CHK + (sg + 1) * 128],
                                    ident_b)
                        if pt % 8 == 7:
                            c_pe.inc(t.last)
                            gidx += 1
                for sg in range(2):
                    bkt = gidx % 2
                    t.wait_ge(c_ve.h, ve_xt0 + gidx - 1)
                    t.transpose(pb[bkt][:, 0:128], Bt_in[:, sg * 128:(sg + 1) * 128],
                                ident_b)
                    c_pe.inc(t.last)
                    gidx += 1

            @sec(V)
            def _(v):
                for gidx in range(10):
                    v.wait_ge(c_pe.h, pe_xt0 + gidx + 1)
                    bkt = gidx % 2
                    if gidx < 8:
                        sg, blk = divmod(gidx, 4)
                        dst = xT_sb[:, sg * DI + blk * 1024:sg * DI + (blk + 1) * 1024]
                        c_ve.inc(v.tensor_copy(out=dst, in_=pb[bkt][:]))
                    else:
                        sg = gidx - 8
                        c_ve.inc(v.tensor_copy(out=BT_sb[:, sg * 128:(sg + 1) * 128],
                                               in_=pb[bkt][:, 0:128]))
            ve_xtE = c_ve.n

            @sec(V)
            def _(v):
                v.wait_ge(c_ve.h, ve_xtE)
                for sg in range(2):
                    for h in range(NH):
                        v.tensor_scalar(out=xds[:, sg * DI + h * 64:sg * DI + (h + 1) * 64],
                                        in0=xT_sb[:, sg * DI + h * 64:sg * DI + (h + 1) * 64],
                                        scalar1=ddT[:, sg * 64 + h:sg * 64 + h + 1],
                                        scalar2=None, op0=ALU.mult)
                c_ve.inc(v.last)
            ve_xds = c_ve.n

            pe_sm0 = c_pe.n
            ve_sm0 = c_ve.n

            @sec(T)
            def _(t):
                t.wait_ge(c_ve.h, ve_xds)
                for hg in range(8):
                    if hg >= 2:
                        t.wait_ge(c_ve.h, ve_sm0 + hg - 1)
                    bank = 1 + (hg % 2)
                    for sg in range(2):
                        t.matmul(pf[bank][:],
                                 BT_sb[:, sg * 128:(sg + 1) * 128],
                                 xds[:, sg * DI + hg * 512:
                                      sg * DI + (hg + 1) * 512],
                                 start=(sg == 0), stop=(sg == 1))
                    c_pe.inc(t.last)

            @sec(V)
            def _(v):
                for hg in range(8):
                    v.wait_ge(c_pe.h, pe_sm0 + hg + 1)
                    bank = 1 + (hg % 2)
                    c_ve.inc(v.tensor_copy(out=states_sb[:, hg * 512:(hg + 1) * 512],
                                           in_=pf[bank][:]))
            ve_smE = c_ve.n

            @sec(Y)
            def _(sy, c=c):
                sy.wait_ge(c_ve.h, ve_xtE)
                sy.wait_ge(c_st.h, c_st.n)
                for sg in range(2):
                    c_st.inc(sy.dma_start(
                        out=xt_dram[c * CHK + sg * 128:c * CHK + (sg + 1) * 128, :],
                        in_=xT_sb[:, sg * DI:(sg + 1) * DI]), 16)
                sy.wait_ge(c_st.h, c_st.n)

            @sec(G)
            def _(g, c=c):
                g.wait_ge(c_ve.h, ve_smE)
                g.wait_ge(c_lg.h, c_lg.n)
                c_lg.inc(g.dma_start(
                    out=st_dram[c * 128:(c + 1) * 128, :], in_=states_sb[:],
                    max_dma_last_dim=4096), 16)
                g.wait_ge(c_lg.h, c_lg.n)

            @sec(V)
            def _(v):
                c_ve.inc(v.tensor_scalar(out=cdb64a[:], in0=ones64_b,
                                         scalar1=cdc[:], scalar2=None, op0=ALU.mult))
            ve_rep = c_ve.n

            pe_cd0 = c_pe.n
            ve_cd0 = c_ve.n

            @sec(T)
            def _(t):
                t.wait_ge(c_ve.h, ve_rep)
                for gq in range(8):
                    if gq >= 2:
                        t.wait_ge(c_ve.h, ve_cd0 + 2 * (gq - 1))
                    bank = 3 + (gq % 2)
                    t.matmul(pf[bank][:], cdb64a[:],
                             rep64_b[:, gq * 512:(gq + 1) * 512],
                             start=True, stop=True, tile_position=TP0)
                    c_pe.inc(t.last)

            @sec(V)
            def _(v):
                for gq in range(8):
                    v.wait_ge(c_pe.h, pe_cd0 + gq + 1)
                    bank = 3 + (gq % 2)
                    v.tensor_tensor(out=S_run[:, gq * 512:(gq + 1) * 512],
                                    in0=S_run[:, gq * 512:(gq + 1) * 512],
                                    in1=pf[bank][:], op=ALU.mult)
                    c_ve.inc(v.last)
                    v.tensor_add(S_run[:, gq * 512:(gq + 1) * 512],
                                 S_run[:, gq * 512:(gq + 1) * 512],
                                 states_sb[:, gq * 512:(gq + 1) * 512])
                    c_ve.inc(v.last)
            barrier()

        @sec(G)
        def _(g):
            c_lg.inc(g.dma_start(out=scc_in.ap(), in_=S_run[:]), 16)
            g.wait_ge(c_lg.h, c_lg.n)
            c_cc2.inc(g.collective_compute(
                "AllGather", ALU.bypass,
                replica_groups=[[0, 1], [2, 3], [4, 5], [6, 7]],
                ins=[scc_in.ap().opt()], outs=[scc_out.ap().opt()]), 1)
            g.wait_ge(c_cc2.h, c_cc2.n)
            c_lg.inc(g.dma_start(out=states_sb[:], in_=scc_out.ap()[0:128, :]), 16)
        lg_s = c_lg.n

        @sec(V)
        def _(v):
            v.wait_ge(c_lg.h, lg_s)
            c_ve.inc(v.tensor_scalar(out=S_run[:], in0=states_sb[:], scalar1=smask,
                                     scalar2=None, op0=ALU.mult))
        barrier()
        s1s.close()

    if "s2" in PHASES:
        s2s = contextlib.ExitStack()
        psb = lambda n, s, d=F32: s2s.enter_context(nc.sbuf_tensor(n, s, d))
        ACq = psb("ACq", [33, 2 * DI], F32)
        EAq = psb("EAq", [33, 2 * DI], BF16)
        xT2 = psb("s2xT", [128, 2 * DI], BF16)
        xch2 = psb("s2xch", [128, 32 * CHK], BF16)
        z_sb = psb("s2z", [128, 32 * CHK], BF16)
        states2 = psb("s2st", [128, DI], BF16)
        prevbf = psb("s2prev", [128, DI], BF16)
        BC_sb = psb("s2BC", [128, 512], BF16)
        CBtm = psb("s2CBtm", [128, 512], BF16)
        G2 = psb("s2G", [128, 2048], BF16)
        expD2 = psb("s2eD", [128, 2048], BF16)
        Cs2 = psb("s2Cs", [128, 1024], BF16)
        ones_t2 = psb("s2ones", [64, CHK])
        ac2 = psb("s2ac", [64, CHK])
        acum2 = psb("s2acum", [128, CHK])
        eAr = psb("s2eA", [64, CHK])
        negAT = psb("s2nAT", [128, 128])
        dtvT = psb("s2dtvT", [128, 128])
        dtvp = psb("s2dtvp", [128, CHK])
        cdc2 = psb("s2cdc", [64, 1])
        cdb64b = psb("s2cdb", [64, 128], BF16)
        yg = psb("s2yg", [128, 32 * CHK], BF16)
        ypair = psb("s2yp", [128, 2 * CHK])
        ysq = xT2
        yn_sb = xch2
        rmsbc = psb("s2rms", [128, 512])

        @sec(V)
        def _(v):
            v.memset(ones_t2[:], 1.0)
            v.memset(acum2[:], 0.0)
            v.memset(dtvp[:], 0.0)
            c_ve.inc(v.last)
        barrier()

        for c in range(NCHK):
            q0 = c * CHK

            @sec(Y)
            def _(sy, c=c, q0=q0):
                sy.wait_ge(c_ld.h, c_ld.n)
                c_ld.inc(sy.dma_start(
                    out=xT2[:],
                    in_=xt_dram[q0:q0 + CHK, :].rearrange("(sg p) n -> p sg n", p=128)), 16)
                c_ld.inc(sy.dma_start(
                    out=xch2[:],
                    in_=xcb_dram[0:DI, q0:q0 + CHK].rearrange("(t p) q -> p t q", p=128)), 16)
                c_ld.inc(sy.dma_start(
                    out=z_sb[:],
                    in_=z_dram[:, q0:q0 + CHK].rearrange("(t p) q -> p t q", p=128)), 16)
                c_ld.inc(sy.dma_start(out=states2[:],
                                      in_=st_dram[c * 128:(c + 1) * 128, :]), 16)
                c_ld.inc(sy.dma_start(out=BC_sb[:, 0:256],
                                      in_=xcb_dram[DI:DI + 128, q0:q0 + CHK]), 16)
                c_ld.inc(sy.dma_start(out=BC_sb[:, 256:512],
                                      in_=xcb_dram[DI + 128:DI + 256, q0:q0 + CHK]), 16)
            ld_c = c_ld.n

            @sec(V)
            def _(v, c=c):
                v.tensor_scalar(out=ac2[:], in0=dtv_all[:, c * CHK:(c + 1) * CHK],
                                scalar1=acol[:], scalar2=None, op0=ALU.mult)
                c_ve.inc(v.tensor_copy(out=dtvp[0:64, :],
                                       in_=dtv_all[:, c * CHK:(c + 1) * CHK]))
                v.wait_ge(c_ve.h, c_ve.n)
                c_ve.inc(v.tensor_tensor_scan(out=acum2[0:64, :], data0=ones_t2[:],
                                              data1=ac2[:], initial=0.0,
                                              op0=ALU.mult, op1=ALU.add))
            ve_ac = c_ve.n

            @sec(S)
            def _(s):
                s.wait_ge(c_ve.h, ve_ac)
                s.activation(out=eAr[:], in_=acum2[0:64, :], func=AF.Exp)
                c_se.inc(s.activation(out=cdc2[:], in_=acum2[0:64, CHK - 1:CHK],
                                      func=AF.Exp))
            se_ea = c_se.n

            @sec(Y)
            def _(sy):
                sy.wait_ge(c_se.h, se_ea)
                sy.wait_ge(c_st.h, c_st.n)
                c_st.inc(sy.dma_start(out=ac_dram, in_=acum2[0:64, :]), 16)
                c_st.inc(sy.dma_start(out=ea_dram, in_=eAr[:]), 16)
                sy.wait_ge(c_st.h, c_st.n)
                sy.wait_ge(c_ld.h, c_ld.n)
                for qd in range(4):
                    r0 = 32 * (qd & 1)
                    cb = DI * (qd >> 1)
                    c_ld.inc(sy.dma_start(
                        out=ACq[r0:r0 + 1, cb:cb + DI],
                        in_=ac_dram[16 * qd:16 * (qd + 1), :]), 16)
            ld_q = c_ld.n

            @sec(G)
            def _(g):
                g.wait_ge(c_st.h, c_st.n)
                g.wait_ge(c_lg.h, c_lg.n)
                for qd in range(4):
                    r0 = 32 * (qd & 1)
                    cb = DI * (qd >> 1)
                    c_lg.inc(g.dma_start(
                        out=EAq[r0:r0 + 1, cb:cb + DI],
                        in_=ea_dram[16 * qd:16 * (qd + 1), :]), 16)
            lg_q = c_lg.n

            @sec(T)
            def _(t):
                t.wait_ge(c_ve.h, ve_ac)
                for sg in range(2):
                    t.transpose(pf[5][:, sg * 128:(sg + 1) * 128],
                                acum2[:, sg * 128:(sg + 1) * 128], ident_f)
                c_pe.inc(t.last)
            pe_tp1 = c_pe.n

            @sec(V)
            def _(v):
                v.wait_ge(c_pe.h, pe_tp1)
                v.tensor_scalar(out=negAT[:, 0:64], in0=pf[5][:, 0:64],
                                scalar1=-1.0, scalar2=None, op0=ALU.mult)
                c_ve.inc(v.tensor_scalar(out=negAT[:, 64:128], in0=pf[5][:, 128:192],
                                         scalar1=-1.0, scalar2=None, op0=ALU.mult))
            ve_tp1 = c_ve.n

            @sec(T)
            def _(t):
                t.wait_ge(c_ve.h, ve_tp1)
                for sg in range(2):
                    t.transpose(pf[5][:, sg * 128:(sg + 1) * 128],
                                dtvp[:, sg * 128:(sg + 1) * 128], ident_f)
                c_pe.inc(t.last)
            pe_tp2 = c_pe.n

            @sec(V)
            def _(v):
                v.wait_ge(c_pe.h, pe_tp2)
                v.tensor_copy(out=dtvT[:, 0:64], in_=pf[5][:, 0:64])
                v.tensor_copy(out=dtvT[:, 64:128], in_=pf[5][:, 128:192])
                c_ve.inc(v.tensor_copy(out=prevbf[:], in_=S_run[:]))
            ve_tp = c_ve.n

            @sec(T)
            def _(t):
                t.wait_ge(c_ld.h, ld_c)
                t.matmul(pf[4][:, 0:256], BC_sb[:, 0:128], BC_sb[:, 256:512],
                         start=True, stop=False)
                t.matmul(pf[4][:, 256:512], BC_sb[:, 128:256], BC_sb[:, 256:512],
                         start=False, stop=True)
                c_pe.inc(t.last)
            pe_cbt = c_pe.n

            @sec(V)
            def _(v):
                v.wait_ge(c_pe.h, pe_cbt)
                c_ve.inc(v.tensor_tensor(out=CBtm[:], in0=pf[4][:], in1=cbmask_b,
                                         op=ALU.mult))
            ve_cbt = c_ve.n

            @sec(S)
            def _(s):
                s.wait_ge(c_ld.h, ld_c)
                c_se.inc(s.activation(out=yg[:], in_=z_sb[:], func=AF.Silu))
            se_siluz = c_se.n

            NJ = 32
            pe_D_l, pe_E_l, se_l, veGa_l, veG_l, pe_Y_l, veY_l = \
                [], [], [], [], [], [], []

            def emit_T1(j):
                h0, h1 = 2 * j, 2 * j + 1
                p = j % 2
                bkA, bkB = (1, 2) if p == 0 else (0, 3)

                @sec(T)
                def _(t, j=j, h0=h0, h1=h1, bkA=bkA, bkB=bkB,
                      w_se=(se_l[j - 2] if j >= 2 else None)):
                    if j == 0:
                        t.wait_ge(c_ld.h, ld_q)
                        t.wait_ge(c_lg.h, lg_q)
                        t.wait_ge(c_ve.h, ve_cbt)
                        t.wait_ge(c_ve.h, ve_tp)
                    if w_se is not None:
                        t.wait_ge(c_se.h, w_se)
                    for idx, hh in enumerate((h0, h1)):
                        qd, qo = divmod(hh, 16)
                        r0 = 32 * (qd & 1)
                        cb = DI * (qd >> 1)
                        bk = bkA if idx == 0 else bkB
                        t.matmul(pf[bk][:, 0:256], ones4(qd),
                                 ACq[r0:r0 + 1, cb + qo * 256:cb + (qo + 1) * 256],
                                 start=True, stop=False)
                        t.matmul(pf[bk][:, 256:512], ones4(qd),
                                 ACq[r0:r0 + 1, cb + qo * 256:cb + (qo + 1) * 256],
                                 start=False, stop=False)
                        t.matmul(pf[bk][:], ident_f, dmask_f[:],
                                 start=False, stop=True)
                    c_pe.inc(t.last)
                pe_D_l.append(c_pe.n)

            emit_T1(0)
            for j in range(NJ):
                h0, h1 = 2 * j, 2 * j + 1
                p = j % 2
                bkA, bkB = (1, 2) if p == 0 else (0, 3)
                bkY = 4 if p == 0 else 5
                if j + 1 < NJ:
                    emit_T1(j + 1)

                @sec(S)
                def _(s, j=j, h0=h0, h1=h1, p=p, bkA=bkA, bkB=bkB,
                      w_ve=(veGa_l[j - 2] if j >= 2 else None)):
                    s.wait_ge(c_pe.h, pe_D_l[j])
                    if w_ve is not None:
                        s.wait_ge(c_ve.h, w_ve)
                    for idx, hh in enumerate((h0, h1)):
                        bk = bkA if idx == 0 else bkB
                        o = p * 1024 + idx * 512
                        for sg in range(2):
                            s.activation(
                                out=expD2[:, o + sg * 256:o + (sg + 1) * 256],
                                in_=pf[bk][:, sg * 256:(sg + 1) * 256],
                                func=AF.Exp,
                                bias=negAT[:, sg * 64 + hh:sg * 64 + hh + 1])
                    c_se.inc(s.last)
                se_l.append(c_se.n)

                @sec(V)
                def _(v, j=j, h0=h0, h1=h1, p=p,
                      w_pe=(pe_Y_l[j - 2] if j >= 2 else None)):
                    v.wait_ge(c_se.h, se_l[j])
                    if w_pe is not None:
                        v.wait_ge(c_pe.h, w_pe)
                    for idx, hh in enumerate((h0, h1)):
                        o = p * 1024 + idx * 512
                        for sg in range(2):
                            v.scalar_tensor_tensor(
                                out=G2[:, o + sg * 256:o + (sg + 1) * 256],
                                in0=expD2[:, o + sg * 256:o + (sg + 1) * 256],
                                scalar=dtvT[:, sg * 64 + hh:sg * 64 + hh + 1],
                                in1=CBtm[:, sg * 256:(sg + 1) * 256],
                                op0=ALU.mult, op1=ALU.mult)
                    c_ve.inc(v.last)
                veGa_l.append(c_ve.n)

                @sec(T)
                def _(t, j=j, h0=h0, h1=h1,
                      w_ve=(veG_l[j - 1] if j >= 1 else None)):
                    if j < 2:
                        t.wait_ge(c_ve.h, ve_tp)
                    if w_ve is not None:
                        t.wait_ge(c_ve.h, w_ve)
                    for idx, hh in enumerate((h0, h1)):
                        qd, qo = divmod(hh, 16)
                        r0 = 32 * (qd & 1)
                        cb = DI * (qd >> 1)
                        t.matmul(pf[5][:, idx * 256:(idx + 1) * 256], onesb4(qd),
                                 EAq[r0:r0 + 1, cb + qo * 256:cb + (qo + 1) * 256],
                                 start=(idx == 0), stop=(idx == 1),
                                 tile_position=(r0, 0))
                    c_pe.inc(t.last)
                pe_E_l.append(c_pe.n)

                @sec(V)
                def _(v, j=j, p=p):
                    v.wait_ge(c_pe.h, pe_E_l[j])
                    for idx in range(2):
                        v.tensor_tensor(out=Cs2[:, p * 512 + idx * 256:
                                                 p * 512 + (idx + 1) * 256],
                                        in0=BC_sb[:, 256:512],
                                        in1=pf[5][:, idx * 256:(idx + 1) * 256],
                                        op=ALU.mult)
                    c_ve.inc(v.last)
                veG_l.append(c_ve.n)

                @sec(T)
                def _(t, j=j, h0=h0, h1=h1, p=p,
                      w_ve=(veY_l[j - 1] if j >= 1 else None)):
                    t.wait_ge(c_ve.h, veG_l[j])
                    if j < 2:
                        t.wait_ge(c_ve.h, ve_tp)
                    if w_ve is not None:
                        t.wait_ge(c_ve.h, w_ve)
                    for idx, hh in enumerate((h0, h1)):
                        base = idx * 64
                        tp = (0, base)
                        o = p * 1024 + idx * 512
                        t.matmul(pf[4][base:base + 64, 0:256],
                                 xT2[:, hh * 64:(hh + 1) * 64],
                                 G2[:, o:o + 256],
                                 start=True, stop=False, tile_position=tp)
                        t.matmul(pf[4][base:base + 64, 0:256],
                                 xT2[:, DI + hh * 64:DI + (hh + 1) * 64],
                                 G2[:, o + 256:o + 512],
                                 start=False, stop=False, tile_position=tp)
                        t.matmul(pf[4][base:base + 64, 0:256],
                                 prevbf[:, hh * 64:(hh + 1) * 64],
                                 Cs2[:, p * 512 + idx * 256:
                                     p * 512 + (idx + 1) * 256],
                                 start=False, stop=True, tile_position=tp)
                    c_pe.inc(t.last)
                pe_Y_l.append(c_pe.n)

                @sec(V)
                def _(v, j=j, p=p):
                    v.wait_ge(c_pe.h, pe_Y_l[j])
                    if j == 0:
                        v.wait_ge(c_se.h, se_siluz)
                    c_ve.inc(v.scalar_tensor_tensor(
                        out=ypair[:, p * 256:(p + 1) * 256],
                        in0=xch2[:, j * 256:(j + 1) * 256],
                        scalar=drep(j), in1=pf[4][:, 0:256],
                        op0=ALU.mult, op1=ALU.add))
                    v.wait_ge(c_ve.h, c_ve.n)
                    v.tensor_tensor(out=yg[:, j * 256:(j + 1) * 256],
                                    in0=ypair[:, p * 256:(p + 1) * 256],
                                    in1=yg[:, j * 256:(j + 1) * 256], op=ALU.mult)
                    c_ve.inc(v.last)
                veY_l.append(c_ve.n)
            ve_yall = c_ve.n

            @sec(S)
            def _(s):
                s.wait_ge(c_ve.h, ve_yall)
                c_se.inc(s.activation(out=ysq[:], in_=yg[:], func=AF.Square))
            se_ysq = c_se.n

            @sec(T)
            def _(t):
                t.wait_ge(c_se.h, se_ysq)
                for pt in range(32):
                    t.matmul(pf[0][:, 0:256], onesSi_b,
                             ysq[:, pt * 256:(pt + 1) * 256],
                             start=(pt == 0), stop=(pt == 31))
                c_pe.inc(t.last)
            pe_v = c_pe.n

            @sec(S)
            def _(s):
                s.wait_ge(c_pe.h, pe_v)
                c_se.inc(s.activation(out=rmsbc[:, 256:512], in_=pf[0][:, 0:256],
                                      func=AF.Ln, bias=eps_col))
                s.wait_ge(c_se.h, c_se.n)
                c_se.inc(s.activation(out=rmsbc[:, 0:256], in_=rmsbc[:, 256:512],
                                      func=AF.Exp, scale=-0.5))
            se_rs = c_se.n

            @sec(V)
            def _(v):
                v.wait_ge(c_se.h, se_rs)
                for pt in range(32):
                    v.scalar_tensor_tensor(out=yn_sb[:, pt * 256:(pt + 1) * 256],
                                           in0=yg[:, pt * 256:(pt + 1) * 256],
                                           scalar=nwcol(pt), in1=rmsbc[:, 0:256],
                                           op0=ALU.mult, op1=ALU.mult)
                c_ve.inc(v.last)
            ve_yn = c_ve.n

            @sec(Y)
            def _(sy, q0=q0):
                sy.wait_ge(c_ve.h, ve_yn)
                sy.wait_ge(c_st.h, c_st.n)
                c_st.inc(sy.dma_start(
                    out=yn_dram[:, q0:q0 + CHK].rearrange("(t p) q -> p t q", p=128),
                    in_=yn_sb[:]), 16)
                sy.wait_ge(c_st.h, c_st.n)

            @sec(V)
            def _(v):
                c_ve.inc(v.tensor_scalar(out=cdb64b[:], in0=ones64_b,
                                         scalar1=cdc2[:], scalar2=None, op0=ALU.mult))
            ve_rep = c_ve.n

            pe_cd0 = c_pe.n

            @sec(T)
            def _(t):
                t.wait_ge(c_ve.h, ve_rep)
                for gq in range(8):
                    if gq >= 2:
                        t.wait_ge(c_ve.h, ve_rep + 2 * (gq - 1))
                    bank = 2 + (gq % 2)
                    t.matmul(pf[bank][:], cdb64b[:],
                             rep64_b[:, gq * 512:(gq + 1) * 512],
                             start=True, stop=True, tile_position=TP0)
                    c_pe.inc(t.last)

            @sec(V)
            def _(v):
                for gq in range(8):
                    v.wait_ge(c_pe.h, pe_cd0 + gq + 1)
                    bank = 2 + (gq % 2)
                    v.tensor_tensor(out=S_run[:, gq * 512:(gq + 1) * 512],
                                    in0=S_run[:, gq * 512:(gq + 1) * 512],
                                    in1=pf[bank][:], op=ALU.mult)
                    c_ve.inc(v.last)
                    v.tensor_add(S_run[:, gq * 512:(gq + 1) * 512],
                                 S_run[:, gq * 512:(gq + 1) * 512],
                                 states2[:, gq * 512:(gq + 1) * 512])
                    c_ve.inc(v.last)
            barrier()
        s2s.close()

    scan_es.close()

    if "outproj" in PHASES:
        ops = contextlib.ExitStack()
        psb = lambda n, s, d=F32: ops.enter_context(nc.sbuf_tensor(n, s, d))
        opw = psb("opw", [128, 16 * 32 * 128], BF16)
        yn2 = psb("opyn", [128, 32 * TC], BF16)
        x_f3 = psb("opx", [128, NKT * TC])

        @sec(G)
        def _(g):
            g.wait_ge(c_lg.h, c_lg.n)
            for dte in range(16):
                srcw = w_op[:, dte * 128:(dte + 1) * 128]
                srcw = srcw.rearrange("(kt p) c -> p kt c", p=128)
                c_lg.inc(g.dma_start(out=opw[:, dte * DI:(dte + 1) * DI], in_=srcw), 16)
        lg_op = c_lg.n
        barrier()

        for tc in range(NTC):
            cc0 = tc * TC

            @sec(Y)
            def _(sy, cc0=cc0):
                sy.wait_ge(c_ld.h, c_ld.n)
                c_ld.inc(sy.dma_start(
                    out=yn2[:],
                    in_=yn_dram[:, cc0:cc0 + TC].rearrange("(t p) q -> p t q", p=128)), 16)
                c_ld.inc(sy.dma_start(
                    out=x_f3[:],
                    in_=x_dram[:, cc0:cc0 + TC].rearrange("(kt p) t -> p kt t", p=128)), 16)
            ld_c = c_ld.n

            pe_op0 = c_pe.n
            ve_op0 = c_ve.n

            @sec(T)
            def _(t):
                t.wait_ge(c_ld.h, ld_c)
                t.wait_ge(c_lg.h, lg_op)
                for dte in range(16):
                    if dte >= 2:
                        t.wait_ge(c_ve.h, ve_op0 + dte - 1)
                    bank = dte % 2
                    for kt in range(32):
                        t.matmul(pf[bank][:],
                                 opw[:, dte * DI + kt * 128:dte * DI + (kt + 1) * 128],
                                 yn2[:, kt * TC:(kt + 1) * TC],
                                 start=(kt == 0), stop=(kt == 31))
                    c_pe.inc(t.last)

            @sec(V)
            def _(v):
                for dte in range(16):
                    v.wait_ge(c_pe.h, pe_op0 + dte + 1)
                    bank = dte % 2
                    c_ve.inc(v.tensor_tensor(out=x_f3[:, dte * TC:(dte + 1) * TC],
                                             in0=x_f3[:, dte * TC:(dte + 1) * TC],
                                             in1=pf[bank][:], op=ALU.add))
            ve_opE = c_ve.n

            @sec(Y)
            def _(sy, cc0=cc0):
                sy.wait_ge(c_ve.h, ve_opE)
                sy.wait_ge(c_st.h, c_st.n)
                c_st.inc(sy.dma_start(
                    out=x_dram[:, cc0:cc0 + TC].rearrange("(kt p) t -> p kt t", p=128),
                    in_=x_f3[:]), 16)
                sy.wait_ge(c_st.h, c_st.n)
            barrier()
        ops.close()

    if "ffn2" in PHASES:
        emit_ffn(x_dram, out_ext, "F2")
    else:
        @sec(Y)
        def _(sy):
            sy.wait_ge(c_st.h, c_st.n)
            c_st.inc(sy.dma_start(out=out_ext, in_=x_dram), 16)
        barrier()

    @sec(Y)
    def _(sy):
        sy.wait_ge(c_st.h, c_st.n)
    barrier()
    es.close()
    return nc


def _host_consts(core_id):
    cf = np.zeros((128, 1024), np.float32)
    cf[:, 2] = EPS
    cf[:, 3] = 1.0
    cf[0:1, 374:502] = 1.0
    cf[:, 512:640] = np.eye(128, dtype=np.float32)
    cf[:, 372] = 1.0 if core_id % 2 == 1 else 0.0
    cf2 = np.zeros((128, 640), np.float32)
    s = np.arange(128)[:, None]
    q = np.arange(256)[None, :]
    cf2[:, 0:256] = np.where(q >= s, 0.0, -1e30)
    cf2[:, 256:512] = np.where(q >= s + 128, 0.0, -1e30)
    cf2[0, 512:640] = 1.0
    cf2[32, 512:640] = 1.0
    cb = np.zeros((128, 8192), np.float32)
    for e in range(4):
        cb[e, e * 128:(e + 1) * 128] = 1.0
    cb[:, 640:768] = 1.0 / 2048
    cb[:, 768:896] = 1.0 / 4096
    cb[:, 1024:1152] = np.eye(128)
    cb[:, 1280:1536] = (q >= s)
    cb[:, 1536:1792] = (q >= s + 128)
    cb[0:64, 1792:1920] = 1.0
    cb[0, 896:1024] = 1.0
    cb[32, 896:1024] = 1.0
    h = np.arange(64)[:, None]
    jj = np.arange(4096)[None, :]
    cb[0:64, 2048:6144] = (jj // 64 == h)
    return cf, cf2, cb


def kernel(L0_attn_u, L0_attn_in_proj_w, L0_attn_conv_w, L0_attn_conv_b,
           L0_attn_dt_bias, L0_attn_A_log, L0_attn_D, L0_attn_norm_w,
           L0_attn_out_proj_w, L0_ffn_gate_w, L0_ffn_w1, L0_ffn_b1,
           L0_ffn_w2, L0_ffn_b2):
    u = np.ascontiguousarray(np.asarray(L0_attn_u, np.float32))
    w_ip = np.ascontiguousarray(np.asarray(L0_attn_in_proj_w, np.float32).T)
    w_op = np.ascontiguousarray(np.asarray(L0_attn_out_proj_w, np.float32).T)
    w1 = [np.ascontiguousarray(np.asarray(L0_ffn_w1[e], np.float32)) for e in range(NE)]
    w2 = [np.ascontiguousarray(np.asarray(L0_ffn_w2[e], np.float32)) for e in range(NE)]
    b2 = np.ascontiguousarray(np.asarray(L0_ffn_b2, np.float32))
    gate = np.asarray(L0_ffn_gate_w, np.float32)
    b1 = np.asarray(L0_ffn_b1, np.float32)
    convw = np.asarray(L0_attn_conv_w, np.float32)
    convb = np.asarray(L0_attn_conv_b, np.float32)
    dtb = np.asarray(L0_attn_dt_bias, np.float32)
    alog = np.asarray(L0_attn_A_log, np.float32)
    Dv = np.asarray(L0_attn_D, np.float32)
    nw = np.asarray(L0_attn_norm_w, np.float32)

    in_maps = []
    for core in range(8):
        b, half = divmod(core, 2)
        uT = np.ascontiguousarray(u[b, half * RT:(half + 1) * RT, :].T)
        cf, cf2, cb = _host_consts(core)
        cf[:, 8:144] = convw.reshape(34, 128, 4).transpose(1, 0, 2).reshape(128, 136)
        cf[:, 144:178] = convb.reshape(34, 128).T
        cf[:, 178:242] = b1.reshape(4 * 16, 128).T
        cf[:, 242:306] = gate.T.reshape(16, 128, 4).transpose(1, 0, 2).reshape(128, 64)
        cf[0:64, 306] = dtb
        cf[0:64, 307] = alog
        cf[:, 308:340] = np.repeat(Dv, 64).reshape(32, 128).T
        cf[:, 340:372] = nw.reshape(32, 128).T
        m = {"u_ext": uT, "w_ip": w_ip, "w_op": w_op,
             "cst_f": cf, "cst_f2": cf2, "cst_b": cb.astype(BF), "b2_ext": b2}
        for e in range(NE):
            m[f"w1_{e}"] = w1[e]
            m[f"w2_{e}"] = w2[e]
        in_maps.append(m)

    nc = build_nc()
    res = run_bass_kernel_spmd(nc, in_maps, core_ids=list(range(8)))
    out = np.empty((4, 4096, DM), np.float32)
    for core in range(8):
        b, half = divmod(core, 2)
        out[b, half * RT:(half + 1) * RT, :] = np.asarray(res.results[core]["out_ext"]).T
    return out
